# version 07 (final)
# speedup vs baseline: 1.2988x; 1.0355x over previous
import numpy as np
import ml_dtypes
from contextlib import ExitStack

import concourse.bass as bass
import concourse.mybir as mybir
from concourse.bass_utils import run_bass_kernel_spmd

F32 = mybir.dt.float32
BF16 = mybir.dt.bfloat16
AF = mybir.ActivationFunctionType
ALU = mybir.AluOpType
AX = mybir.AxisListType

NCORES = 8
B, T, D = 16, 2048, 1024
BL = B // NCORES
H, DH, DM = 16, 64, 8
NT = T // 128
ND = D // 128
EPS_LN = 1e-5
EPS_PSD = 1e-3
EPS_RMS = 1e-12
NEG_BIG = 1.0e30

bf = ml_dtypes.bfloat16

SKIP_TAGS = set()

PHASES = ["consts", "ln", "vproj", "qkrope", "qm", "a2", "g", "kk", "gqA", "gqB", "gqC", "gqD", "gqE", "gq", "asm", "sc", "exp", "av", "avn", "wo"]
STOP_AFTER = None


def _phlim():
    return len(PHASES) if STOP_AFTER is None else PHASES.index(STOP_AFTER) + 1



ENGS = ("tensor", "vector", "scalar", "gpsimd", "sync")


class _Step:
    __slots__ = ("engine", "fn", "deps", "amt", "hazard", "tag", "needs_inc",
                 "cum_at", "waits", "sem", "val", "selfwait", "idx")

    def __init__(self, engine, fn, deps, amt, hazard, tag):
        self.idx = -1
        self.engine = engine
        self.fn = fn
        self.deps = deps
        self.amt = amt
        self.hazard = hazard
        self.tag = tag
        self.needs_inc = False


class Trk:
    __slots__ = ("w", "r")

    def __init__(self):
        self.w = {}
        self.r = {}

    def rd(self):
        return list(self.w.values())

    def wr(self):
        return list(self.w.values()) + list(self.r.values())

    def did_read(self, s):
        self.r[s.engine] = s

    def did_write(self, s):
        self.w[s.engine] = s
        self.r = {}

    def fork(self):
        t = Trk()
        t.w = dict(self.w)
        t.r = dict(self.r)
        return t

    def absorb(self, others):
        for o in others:
            for eng, st in o.w.items():
                if eng not in self.w or self.w[eng].idx < st.idx:
                    self.w[eng] = st
            for eng, st in o.r.items():
                if eng not in self.r or self.r[eng].idx < st.idx:
                    self.r[eng] = st


class Prog:
    def __init__(self, nc):
        self.nc = nc
        self.steps = []
        self.last = {}
        self.last_dma = {}

    def raw_step(self, engine, fn, deps=(), dma=False, hazard=False, tag=""):
        s = _Step(engine, fn, [d for d in deps if d is not None],
                  16 if dma else 1, hazard, tag)
        s.idx = len(self.steps)
        self.steps.append(s)
        self.last[engine] = s
        if dma:
            q = self.last_dma.setdefault(engine, [])
            q.append(s)
            if len(q) > Prog.NPOOL:
                q.pop(0)
        return s

    def barrier(self):
        lasts = list(self.last.values())
        for q in self.last_dma.values():
            lasts += q
        out = {}
        for e in ENGS:
            out[e] = self.raw_step(e, lambda eng: eng.nop(), deps=lasts,
                                   hazard=True, tag="barrier")
        return out

    def st(self, engine, fn, reads=(), writes=(), extra=(), dma=False,
           hazard=False, tag=""):
        if tag.rstrip("0123456789_") in SKIP_TAGS or tag in SKIP_TAGS:
            return None
        deps = []
        for b in reads:
            deps += b.rd()
        for b in writes:
            deps += b.wr()
        deps += [d for d in extra if d is not None]
        s = self.raw_step(engine, fn, deps, dma=dma, hazard=hazard, tag=tag)
        for b in reads:
            b.did_read(s)
        for b in writes:
            b.did_write(s)
        return s

    NPOOL = 10

    def emit(self, block, sems, dma_pools):
        for s in self.steps:
            if s.amt == 16:
                s.needs_inc = True
            for d in s.deps:
                if d.engine != s.engine or d.hazard:
                    d.needs_inc = True
        cum = {e: 0 for e in ENGS}
        dcnt = {e: 0 for e in ENGS}
        finals = {}
        for s in self.steps:
            if s.amt == 16:
                pool = dma_pools[s.engine]
                slot = dcnt[s.engine] % len(pool)
                use = dcnt[s.engine] // len(pool) + 1
                dcnt[s.engine] += 1
                s.sem = pool[slot]
                s.val = 16 * use
                s.selfwait = 16 * (use - 1)
                finals[id(s.sem)] = (s.sem, max(finals.get(id(s.sem), (None, 0))[1], s.val))
            else:
                s.sem = sems[s.engine]
                if s.needs_inc:
                    cum[s.engine] += 1
                s.val = cum[s.engine]
                s.selfwait = 0
                if s.needs_inc:
                    finals[id(s.sem)] = (s.sem, s.val)
        hw = {e: {} for e in ENGS}
        for s in self.steps:
            req = {}
            for d in s.deps:
                if d.engine == s.engine and not d.hazard and d.amt != 16:
                    continue
                if d.engine == s.engine and d.amt == 16 and d.sem is s.sem and False:
                    continue
                assert d.needs_inc
                k = id(d.sem)
                if k not in req or req[k][1] < d.val:
                    req[k] = (d.sem, d.val)
            s.waits = []
            for k, (sm, v) in req.items():
                if v > hw[s.engine].get(k, 0):
                    s.waits.append((sm, v))
                    hw[s.engine][k] = v
            if s.amt == 16 and s.selfwait > 0:
                if s.selfwait > hw[s.engine].get(id(s.sem), 0):
                    s.waits.append((s.sem, s.selfwait))
                    hw[s.engine][id(s.sem)] = s.selfwait
        per_eng = {e: [s for s in self.steps if s.engine == e] for e in ENGS}

        def make_body(e):
            def body(eng):
                for s in per_eng[e]:
                    for (sm, v) in s.waits:
                        eng.wait_ge(sm, v)
                    inst = s.fn(eng)
                    if s.needs_inc:
                        assert inst is not None, f"step '{s.tag}' on {e} must return its last instruction"
                        inst.then_inc(s.sem, s.amt)
                if e == "sync":
                    for (sm, v) in finals.values():
                        eng.wait_ge(sm, v)
            return body

        for e in ENGS:
            if per_eng[e] or e == "sync":
                getattr(block, e)(make_body(e))
        n_incs = sum(1 for s in self.steps if s.needs_inc)
        return n_incs



def host_constants():
    c = {}
    c["eye_bf"] = np.eye(128, dtype=bf)
    c["eye_f32"] = np.eye(128, dtype=np.float32)
    R = np.zeros((64, 64), np.float32)
    for d in range(32):
        R[d, d + 32] = -1.0
        R[d + 32, d] = 1.0
    R2 = np.zeros((128, 128), np.float32)
    R2[:64, :64] = R
    R2[64:, 64:] = R
    c["R2T_bf"] = np.ascontiguousarray(R2.T).astype(bf)
    inv_freq = 1.0 / 10000.0 ** (np.arange(0, DH, 2, dtype=np.float32) / DH)
    t = np.arange(T, dtype=np.float32)
    freqs = np.outer(inv_freq, t)
    c1 = np.cos(freqs).astype(np.float32)
    s1 = np.sin(freqs).astype(np.float32)
    c["COS_bf"] = np.concatenate([c1, c1, c1, c1], 0).astype(bf)
    c["SIN_bf"] = np.concatenate([s1, s1, s1, s1], 0).astype(bf)
    selp = np.zeros((8, 128, 128), np.float32)
    selq = np.zeros((8, 128, 128), np.float32)
    for e in range(8):
        for hh in range(2):
            for d in range(8):
                for f in range(8):
                    m = hh * 64 + d * 8 + f
                    selp[e, hh * 64 + d * 8 + e, m] = 1.0
                    selq[e, hh * 64 + f * 8 + e, m] = 1.0
    c["SELP_bf"] = selp.transpose(1, 0, 2).reshape(128, 8 * 128).astype(bf)
    c["SELQ_bf"] = selq.transpose(1, 0, 2).reshape(128, 8 * 128).astype(bf)
    selpq = (selp + selq) * np.float32(1.0 / np.sqrt(2.0))
    c["SELPQ_bf"] = selpq.transpose(1, 0, 2).reshape(128, 8 * 128).astype(bf)
    selps = np.zeros((128, 128), np.float32)
    selqs = np.zeros((128, 128), np.float32)
    for hh in range(2):
        for d in range(8):
            for f in range(8):
                m = hh * 64 + d * 8 + f
                for e in range(8):
                    selps[hh * 64 + d * 8 + e, m] = 0.5
                    selqs[hh * 64 + f * 8 + e, m] = 0.5
    c["SELPS_bf"] = selps.astype(bf)
    c["SELQS_bf"] = selqs.astype(bf)
    kkp = np.zeros((128, 128), np.float32)
    kkq = np.zeros((128, 128), np.float32)
    for hh in range(2):
        for d in range(8):
            for f in range(8):
                m = hh * 64 + d * 8 + f
                kkp[hh * 64 + d, m] = 1.0
                kkq[hh * 64 + f, m] = 1.0
    c["KKP_bf"] = kkp.astype(bf)
    c["KKQ_bf"] = kkq.astype(bf)
    grp = np.zeros((128, 40), np.float32)
    for hh in range(2):
        for d in range(8):
            for f in range(8):
                grp[hh * 64 + d * 8 + f, hh * 32 + d] = 1.0
    c["GRP_bf"] = grp.astype(bf)
    epsi = np.zeros((128, 40), np.float32)
    for hh in range(2):
        for d in range(8):
            epsi[hh * 64 + d, hh * 32 + d] = EPS_PSD
    c["EPSI_bf"] = epsi.astype(bf)
    c["ones_row_bf"] = np.ones((1, 128), dtype=bf)
    c["ones_col_bf"] = np.ones((128, 1), dtype=bf)
    epscol = np.zeros((64, 1), np.float32)
    for d in range(8):
        epscol[d * 8 + d, 0] = EPS_PSD
    c["epscol"] = epscol
    epscol128 = np.zeros((128, 1), np.float32)
    for hh in range(2):
        for d in range(8):
            epscol128[hh * 64 + d * 8 + d, 0] = EPS_PSD
    c["epscol128"] = epscol128
    masks = np.zeros((4, 128, 512), np.float32)
    for m in range(4):
        for p in range(128):
            masks[m, p, : min(512, 128 * m + p)] = NEG_BIG
    c["masks"] = masks.reshape(4 * 128, 512).astype(bf)
    mk1 = np.zeros((4, 128, 512), np.float32)
    for m in range(4):
        for p in range(128):
            mk1[m, p, 128 * m + p + 1:] = -NEG_BIG
    c["masksP1"] = mk1.reshape(4 * 128, 512).astype(bf)
    mk2 = np.zeros((4, 128, 512), np.float32)
    for m in range(4):
        for p in range(128):
            mk2[m, p, : min(512, 128 * m + p)] = -NEG_BIG
    c["masksP2"] = mk2.reshape(4 * 128, 512).astype(bf)
    c["ones_2048_f32"] = np.ones((1, T), np.float32)
    c["ones_2048_bf"] = np.ones((1, T), dtype=bf)
    return c


CONST_SPECS = [
    ("eye_bf", BF16), ("eye_f32", F32), ("R2T_bf", BF16), ("COS_bf", BF16),
    ("SIN_bf", BF16), ("SELPQ_bf", BF16), ("SELPS_bf", BF16),
    ("SELQS_bf", BF16), ("KKP_bf", BF16),
    ("KKQ_bf", BF16), ("GRP_bf", BF16), ("EPSI_bf", BF16),
    ("ones_row_bf", BF16), ("ones_col_bf", BF16), ("epscol", F32), ("epscol128", F32),
    ("masks", BF16), ("masksP1", BF16), ("masksP2", BF16),
    ("ones_2048_f32", F32), ("ones_2048_bf", BF16),
]



class Ctx:

    def __init__(self, name="prog"):
        self.nc = bass.Bass("TRN2", target_bir_lowering=False, name=name,
                            detect_race_conditions=False)
        self.P = Prog(self.nc)
        self.stack = ExitStack()
        self.trk = {}
        self.handles = {}
        self.phase_idx = 0
        self.phase_stack = None
        self.cur_barrier = None

    def _mk_trk(self):
        t = Trk()
        if self.cur_barrier is not None:
            t.w = dict(self.cur_barrier)
        return t

    def begin_phase(self):
        if self.phase_stack is not None:
            self.end_phase()
        self.phase_idx += 1
        self.phase_stack = ExitStack()
        if self.phase_idx > 1:
            self.cur_barrier = self.P.barrier()

    def end_phase(self):
        if self.phase_stack is not None:
            self.phase_stack.close()
            self.phase_stack = None

    def sb(self, name, shape, dt=F32):
        stk = self.phase_stack if self.phase_stack is not None else self.stack
        bass_name = name if self.phase_idx == 0 else f"{name}_p{self.phase_idx}"
        h = stk.enter_context(self.nc.sbuf_tensor(bass_name, shape, dt))
        self.trk[name] = self._mk_trk()
        self.handles[name] = h
        return h

    def ps(self, name, shape, dt=F32):
        stk = self.phase_stack if self.phase_stack is not None else self.stack
        bass_name = name if self.phase_idx == 0 else f"{name}_p{self.phase_idx}"
        h = stk.enter_context(self.nc.psum_tensor(bass_name, shape, dt))
        self.trk[name] = self._mk_trk()
        self.handles[name] = h
        return h

    def dram_in(self, name, shape, dt=F32):
        return self.nc.dram_tensor(name, shape, dt, kind="ExternalInput").ap()

    def dram_out(self, name, shape, dt=F32):
        return self.nc.dram_tensor(name, shape, dt, kind="ExternalOutput").ap()

    def t(self, h):
        if isinstance(h, str):
            return self.trk[h]
        return self.trk[h.name if hasattr(h, "name") else h]

    def finish(self):
        self.end_phase()
        sems = {}
        for e in ENGS:
            sems[e] = self.stack.enter_context(self.nc.semaphore(f"sem_{e}"))
        dma_pools = {}
        for e in ("sync", "gpsimd", "scalar", "vector", "tensor"):
            np_ = Prog.NPOOL if e in ("sync", "gpsimd", "scalar") else 1
            dma_pools[e] = [self.stack.enter_context(self.nc.semaphore(f"dq_{e}{i}"))
                            for i in range(np_)]
        block = self.stack.enter_context(self.nc.Block())
        n_incs = self.P.emit(block, sems, dma_pools)
        self.stack.close()
        return n_incs


def declare_consts(cx, consts):
    aps = {}
    for name, dt in CONST_SPECS:
        arr = consts[name]
        aps[name] = cx.dram_in("c_" + name, list(arr.shape), dt)
    return aps


def const_inmap(consts):
    return {"c_" + name: consts[name] for name, _ in CONST_SPECS}



def emit_l0_attn(cx, caps, x_sh, wq, wk, wv, wo, wqm, wkm, wmetric, temp,
                 attn0T_out, xn0T_all=None, stats_out=None, n_batches=BL, n_pairs=H // 2):
    nc, P = cx.nc, cx.P
    sb, ps = cx.sb, cx.ps

    eye_b = sb("eye_b", [128, 128], BF16)
    r2t = sb("r2t", [128, 128], BF16)
    cosb = sb("cosb", [128, T], BF16)
    sinb = sb("sinb", [128, T], BF16)
    selpq = sb("selpq", [128, 8 * 128], BF16)
    selps = sb("selps", [128, 128], BF16)
    selqs = sb("selqs", [128, 128], BF16)
    kkp = sb("kkp", [128, 128], BF16)
    kkq = sb("kkq", [128, 128], BF16)
    grp = sb("grp", [128, 40], BF16)
    epsi = sb("epsi", [128, 40], BF16)
    onesr = sb("onesr", [1, 128], BF16)
    onesc = sb("onesc", [128, 1], BF16)
    epsc = sb("epsc", [64, 1], F32)
    epsc128 = sb("epsc128", [128, 1], F32)
    maskb = sb("maskb", [128, 4 * 512], BF16)
    wqmd = sb("wqmd", [128, 8], BF16)
    wkmd = sb("wkmd", [128, 8], BF16)
    wmd = sb("wmd", [128, 64], BF16)
    tmpc = sb("tmpc", [1, 4], F32)
    nit = sb("nit", [128, 1], F32)

    for name, buf, tn in [("eye_bf", eye_b, "eye_b"), ("R2T_bf", r2t, "r2t"),
                          ("COS_bf", cosb, "cosb"), ("SIN_bf", sinb, "sinb"),
                          ("SELPQ_bf", selpq, "selpq"), ("SELPS_bf", selps, "selps"),
                          ("SELQS_bf", selqs, "selqs"),
                          ("KKP_bf", kkp, "kkp"), ("KKQ_bf", kkq, "kkq"),
                          ("GRP_bf", grp, "grp"), ("EPSI_bf", epsi, "epsi"),
                          ("ones_row_bf", onesr, "onesr"),
                          ("ones_col_bf", onesc, "onesc"), ("epscol", epsc, "epsc"),
                          ("epscol128", epsc128, "epsc128")]:
        ap = caps[name]
        P.st("sync", lambda e, b2=buf, a=ap: e.dma_start(out=b2[:, :], in_=a),
             writes=[cx.t(tn)], dma=True, tag=f"ld_{name}")
    P.st("sync", lambda e: e.dma_start(
        out=maskb[:, :].rearrange("p (m f) -> p m f", m=4),
        in_=caps["masks"].rearrange("(m p) f -> p m f", m=4)),
        writes=[cx.t("maskb")], dma=True, tag="ld_masks")
    for (src, buf, nm) in [(wqm, wqmd, "wqmd"), (wkm, wkmd, "wkmd")]:
        P.st("gpsimd", lambda e, b2=buf, s2=src: e.dma_start(out=b2[0:64, :], in_=s2),
             writes=[cx.t(nm)], dma=True, tag=f"ld_{nm}0")
        P.st("gpsimd", lambda e, b2=buf, s2=src: e.dma_start(out=b2[64:128, :], in_=s2),
             writes=[cx.t(nm)], dma=True, tag=f"ld_{nm}1")
    P.st("gpsimd", lambda e: e.dma_start(out=wmd[0:8, :], in_=wmetric),
         writes=[cx.t("wmd")], dma=True, tag="ld_wmd0")
    P.st("gpsimd", lambda e: e.dma_start(out=wmd[64:72, :], in_=wmetric),
         writes=[cx.t("wmd")], dma=True, tag="ld_wmd1")
    P.st("sync", lambda e: e.dma_start(out=tmpc[0:1, 0:1], in_=temp),
         writes=[cx.t("tmpc")], dma=True, tag="ld_temp")
    P.st("vector", lambda e: e.tensor_scalar_max(tmpc[0:1, 1:2], tmpc[0:1, 0:1], 0.1),
         reads=[cx.t("tmpc")], writes=[cx.t("tmpc")], hazard=True, tag="tmax")
    P.st("vector", lambda e: e.reciprocal(tmpc[0:1, 2:3], tmpc[0:1, 1:2]),
         reads=[cx.t("tmpc")], writes=[cx.t("tmpc")], hazard=True, tag="trec")
    P.st("vector", lambda e: e.tensor_scalar_mul(tmpc[0:1, 3:4], tmpc[0:1, 2:3], -1.0),
         reads=[cx.t("tmpc")], writes=[cx.t("tmpc")], hazard=True, tag="tneg")
    tmpb = sb("tmpb_c", [1, 1], BF16)
    P.st("vector", lambda e: e.tensor_copy(tmpb[0:1, 0:1], tmpc[0:1, 3:4]),
         reads=[cx.t("tmpc")], writes=[cx.t("tmpb_c")], tag="tcast")
    TEMP_REPL = []

    xt = [sb("xt0", [128, D], F32)]
    scr = sb("scr", [128, T], F32)
    xn0b = sb("xn0b", [128, D], BF16)
    ncol = 8
    C = {}
    for nm in ["c_sum", "c_m", "c_v", "c_u", "c_rstd", "c_nm", "c_sq", "c_t"]:
        C[nm] = sb(nm, [128, 1], F32)
    if xn0T_all is None:
        xn0T = sb("xn0T", [128, ND * T], BF16)
        xn0T_stride = 0
    else:
        xn0T = xn0T_all
        xn0T_stride = ND * T
    v_aug = sb("v_aug", [128, NT * (H * 65 + 0)], BF16)
    avT = sb("avT", [128, ND * T], BF16)
    wvb = sb("wvb", [128, ND * D], BF16)
    wqc = [sb("wqc0", [128, ND * 128], BF16)]
    wkc = [sb("wkc0", [128, ND * 128], BF16)]
    ctile = sb("ctile", [128, T], BF16)
    roped = sb("roped", [128, T], BF16)
    qm2 = sb("qm2", [128, T], BF16)
    km2 = sb("km2", [128, T], BF16)
    gacc = sb("gacc", [128, T], BF16)
    gtmp = sb("gtmp", [128, 512], BF16)
    sqf = [sb("sqf0", [128, 512], F32), sb("sqf1", [128, 512], F32)]
    kk2b = sb("kk2b", [128, T], BF16)
    gqs = sb("gqs", [40, T], BF16)
    t1b = sb("t1b", [128, 512], BF16)
    t2b = sb("t2b", [40, 512], BF16)
    kkh = [sb("kkh0", [128, T], BF16), sb("kkh1", [128, T], BF16)]
    qgh = [sb("qgh0", [128, T], BF16), sb("qgh1", [128, T], BF16)]
    prT = [sb("prT0", [128, T], BF16), sb("prT1", [128, T], BF16)]

    pm = [ps("pm0", [128, 512], F32), ps("pm1", [128, 512], F32)]
    pqm = ps("pqm", [128, 512], F32)
    pbig = ps("pbig", [128, T], F32)
    ptb = ps("ptb", [128, 1024], BF16)

    P.st("vector", lambda e: e.memset(pqm[:, :], 0.0), writes=[cx.t("pqm")], tag="pqm0")
    P.st("vector", lambda e: e.memset(avT[:, :], 0.0), writes=[cx.t("avT")], tag="avT0")
    P.st("tensor", lambda e: e.matmul(pbig[:, 0:1], onesr[0:1, :], tmpb[0:1, 0:1],
                                      start=True, stop=True),
         reads=[cx.t("onesr"), cx.t("tmpb_c")], writes=[cx.t("pbig")], tag="trepl")
    P.st("vector", lambda e: e.tensor_copy(nit[:, :], pbig[:, 0:1]),
         reads=[cx.t("pbig")], writes=[cx.t("nit")], tag="tcol")

    for bi in range(2):
        P.st("vector", lambda e, b2=bi: e.memset(kkh[b2][:, :], 0.0),
             writes=[cx.t(f"kkh{bi}")], tag="kkh0")
        P.st("vector", lambda e, b2=bi: e.memset(qgh[b2][:, :], 0.0),
             writes=[cx.t(f"qgh{bi}")], tag="qgh0")
        P.st("sync", lambda e, b2=bi: e.dma_start(
            out=kkh[b2][96:97, :], in_=caps["ones_2048_bf"]),
            writes=[cx.t(f"kkh{bi}")], dma=True, tag="kkh_ones")
    P.st("vector", lambda e: e.memset(
        v_aug[:, :].rearrange("p (s h c) -> p s h c", s=NT, h=H)[:, :, :, 64:65], 1.0),
        writes=[cx.t("v_aug")], tag="vaug_ones")

    kx = [cx.t("xt0")]
    kscr, kxn0b = cx.t("scr"), cx.t("xn0b")
    kxn0T = cx.t("xn0T") if xn0T_all is None else xn0T_all.kk
    kvaug, kavT, kwvb = cx.t("v_aug"), cx.t("avT"), cx.t("wvb")
    kwqc = [cx.t("wqc0")]
    kwkc = [cx.t("wkc0")]
    kct, krp = cx.t("ctile"), cx.t("roped")
    a2b = ctile
    kqm, kkm, ka2, kg = cx.t("qm2"), cx.t("km2"), cx.t("ctile"), cx.t("gacc")
    kgt, kkk2, kgqs = cx.t("gtmp"), cx.t("kk2b"), cx.t("gqs")
    ksqf = [cx.t("sqf0"), cx.t("sqf1")]
    kt1, kt2 = cx.t("t1b"), cx.t("t2b")
    kKK = [cx.t("kkh0"), cx.t("kkh1")]
    kQG = [cx.t("qgh0"), cx.t("qgh1")]
    kpr = [cx.t("prT0"), cx.t("prT1")]
    kpm = [cx.t("pm0"), cx.t("pm1")]
    kpqm, kpbig, kptb = cx.t("pqm"), cx.t("pbig"), cx.t("ptb")
    kC = {nm: cx.t(nm) for nm in C}
    knit = cx.t("nit")
    keye, kmask = cx.t("eye_b"), cx.t("maskb")

    LIM = _phlim()
    for b in range(n_batches):
        xoff = b * T
        if LIM <= PHASES.index("ln"):
            break
        for st_ in range(NT):
            xb_ = xt[0]
            kxb = kx[0]
            P.st("sync", lambda e, s=st_, xb2=xb_, xo=xoff: e.dma_start(
                out=xb2[:, :], in_=x_sh[xo + s * 128: xo + (s + 1) * 128, :]),
                writes=[kxb], dma=True, tag=f"xld{b}_{st_}")
            P.st("vector", lambda e, xb2=xb_: e.reduce_sum(C["c_sum"][:, :], xb2[:, :], AX.X),
                 reads=[kxb], writes=[kC["c_sum"]], hazard=True, tag="lnsum")
            P.st("scalar", lambda e, xb2=xb_: e.activation(
                scr[:, 0:D], xb2[:, :], AF.Square, accum_out=C["c_sq"][:, :]),
                reads=[kxb], writes=[kscr, kC["c_sq"]], hazard=True, tag="lnsq")
            P.st("vector", lambda e: e.tensor_scalar_mul(C["c_m"][:, :], C["c_sum"][:, :], 1.0 / D),
                 reads=[kC["c_sum"]], writes=[kC["c_m"]], hazard=True, tag="lnm")
            P.st("vector", lambda e: e.tensor_scalar_mul(C["c_v"][:, :], C["c_sq"][:, :], 1.0 / D),
                 reads=[kC["c_sq"]], writes=[kC["c_v"]], hazard=True, tag="lnv1")
            P.st("vector", lambda e: e.tensor_mul(C["c_t"][:, :], C["c_m"][:, :], C["c_m"][:, :]),
                 reads=[kC["c_m"]], writes=[kC["c_t"]], hazard=True, tag="lnm2")
            P.st("vector", lambda e: e.tensor_sub(C["c_v"][:, :], C["c_v"][:, :], C["c_t"][:, :]),
                 reads=[kC["c_t"], kC["c_v"]], writes=[kC["c_v"]], hazard=True, tag="lnv2")
            P.st("vector", lambda e: e.tensor_scalar_add(C["c_v"][:, :], C["c_v"][:, :], EPS_LN),
                 reads=[kC["c_v"]], writes=[kC["c_v"]], hazard=True, tag="lnv3")
            P.st("scalar", lambda e: e.activation(C["c_u"][:, :], C["c_v"][:, :], AF.Ln),
                 reads=[kC["c_v"]], writes=[kC["c_u"]], hazard=True, tag="lnln")
            P.st("scalar", lambda e: e.activation(C["c_rstd"][:, :], C["c_u"][:, :], AF.Exp, scale=-0.5),
                 reads=[kC["c_u"]], writes=[kC["c_rstd"]], hazard=True, tag="lnexp")
            P.st("vector", lambda e: e.tensor_mul(C["c_nm"][:, :], C["c_m"][:, :], C["c_rstd"][:, :]),
                 reads=[kC["c_m"], kC["c_rstd"]], writes=[kC["c_nm"]], hazard=True, tag="lnnm")
            P.st("vector", lambda e: e.tensor_scalar_mul(C["c_nm"][:, :], C["c_nm"][:, :], -1.0),
                 reads=[kC["c_nm"]], writes=[kC["c_nm"]], hazard=True, tag="lnnm2")
            P.st("scalar", lambda e, xb2=xb_: e.activation(
                xn0b[:, :], xb2[:, :], AF.Identity,
                bias=C["c_nm"][:, :], scale=C["c_rstd"][:, :]),
                reads=[kxb, kC["c_nm"], kC["c_rstd"]], writes=[kxn0b], tag="lnxn")
            if stats_out is not None:
                P.st("vector", lambda e, s=st_, b2=b: e.tensor_copy(
                    stats_out[0][:, b2 * 2 * NT + 2 * s: b2 * 2 * NT + 2 * s + 1], C["c_rstd"][:, :]),
                    reads=[kC["c_rstd"]], writes=[stats_out[1]], tag="strstd")
                P.st("vector", lambda e, s=st_, b2=b: e.tensor_copy(
                    stats_out[0][:, b2 * 2 * NT + 2 * s + 1: b2 * 2 * NT + 2 * s + 2], C["c_v"][:, :]),
                    reads=[kC["c_v"]], writes=[stats_out[1]], tag="stvar")
            for kd in range(ND):
                P.st("tensor", lambda e, k=kd: e.transpose(
                    ptb[:, k * 128:(k + 1) * 128], xn0b[:, k * 128:(k + 1) * 128], eye_b[:, :]),
                    reads=[kxn0b, keye], writes=[kptb], tag="lntp")
            P.st("vector", lambda e, s=st_: e.tensor_copy(
                xn0T[:, xn0T_stride * b:].rearrange("p (k t) -> p k t", k=ND)[:, :, s * 128:(s + 1) * 128],
                ptb[:, :].rearrange("p (k t) -> p k t", k=ND)),
                reads=[kptb], writes=[kxn0T], tag="lntc")

        def xn0T_sl(kd, lo, hi, b2=b):
            return xn0T[:, xn0T_stride * b2 + kd * T + lo: xn0T_stride * b2 + kd * T + hi]

        if LIM <= PHASES.index("vproj"):
            continue
        P.st("gpsimd", lambda e: e.dma_start(
            out=wvb[:, :].rearrange("p (k c) -> p k c", k=ND),
            in_=wv.rearrange("(k p) c -> p k c", p=128)),
            writes=[kwvb], dma=True, tag=f"wvld{b}")
        for st_ in range(NT):
            for cc2 in range(2):
                pt, kpt = pm[cc2], kpm[cc2]
                def vproj(e, s=st_, c2=cc2, pt2=pt):
                    for kd in range(ND):
                        i = e.matmul(pt2[:, :], xn0T_sl(kd, s * 128, (s + 1) * 128),
                                     wvb[:, kd * D + c2 * 512:(kd) * D + (c2 + 1) * 512],
                                     start=(kd == 0), stop=(kd == ND - 1))
                    return i
                P.st("tensor", vproj, reads=[kxn0T, kwvb], writes=[kpt], tag=f"vproj{b}_{st_}_{cc2}")
                P.st("vector", lambda e, s=st_, c2=cc2, pt2=pt: e.tensor_copy(
                    v_aug[:, s * 1040: (s + 1) * 1040].rearrange(
                        "p (h c) -> p h c", h=H)[:, c2 * 8:(c2 + 1) * 8, 0:64],
                    pt2[:, :].rearrange("p (h c) -> p h c", h=8)),
                    reads=[kpt], writes=[kvaug], tag=f"vevac{b}_{st_}_{cc2}")

        if LIM <= PHASES.index("qkrope"):
            continue
        for p in range(n_pairs):
            wqb, kwqb = wqc[0], kwqc[0]
            wkb, kwkb = wkc[0], kwkc[0]
            P.st("gpsimd", lambda e, pp=p, w=wqb: e.dma_start(
                out=w[:, :].rearrange("p (k c) -> p k c", k=ND),
                in_=wq.rearrange("(k p) c -> p k c", p=128)[:, :, pp * 128:(pp + 1) * 128]),
                writes=[kwqb], dma=True, tag=f"wqld{b}_{p}")
            P.st("gpsimd", lambda e, pp=p, w=wkb: e.dma_start(
                out=w[:, :].rearrange("p (k c) -> p k c", k=ND),
                in_=wk.rearrange("(k p) c -> p k c", p=128)[:, :, pp * 128:(pp + 1) * 128]),
                writes=[kwkb], dma=True, tag=f"wkld{b}_{p}")

            for (wb_, kwb, rp, krp_, qmb, kqmb, wmdup, kwmdup) in [
                    (wqb, kwqb, roped, krp, qm2, kqm, wqmd, cx.t("wqmd")),
                    (wkb, kwkb, roped, krp, km2, kkm, wkmd, cx.t("wkmd"))]:
                for cc in range(4):
                    pt, kpt = pm[cc % 2], kpm[cc % 2]
                    def qproj(e, c=cc, w=wb_, pt2=pt):
                        for kd in range(ND):
                            i = e.matmul(pt2[:, :], w[:, kd * 128:(kd + 1) * 128],
                                         xn0T_sl(kd, c * 512, (c + 1) * 512),
                                         start=(kd == 0), stop=(kd == ND - 1))
                        return i
                    P.st("tensor", qproj, reads=[kxn0T, kwb], writes=[kpt], tag="qproj")
                    P.st("vector", lambda e, c=cc, pt2=pt: e.tensor_copy(
                        ctile[:, c * 512:(c + 1) * 512], pt2[:, :]),
                        reads=[kpt], writes=[kct], tag="qevac")
                    P.st("tensor", lambda e, c=cc: e.matmul(
                        pbig[:, c * 512:(c + 1) * 512], r2t[:, :],
                        ctile[:, c * 512:(c + 1) * 512],
                        start=True, stop=True),
                        reads=[kct, cx.t("r2t")], writes=[kpbig], tag="qrot")
                    P.st("vector", lambda e, c=cc, r=rp: e.tensor_tensor(
                        r[:, c * 512:(c + 1) * 512], ctile[:, c * 512:(c + 1) * 512],
                        cosb[:, c * 512:(c + 1) * 512], ALU.mult),
                        reads=[kct, cx.t("cosb")], writes=[krp_], tag="qcos")
                    P.st("vector", lambda e, c=cc: e.tensor_tensor(
                        xn0b[:, 0:512], pbig[:, c * 512:(c + 1) * 512],
                        sinb[:, c * 512:(c + 1) * 512], ALU.mult),
                        reads=[kpbig, cx.t("sinb")], writes=[kxn0b], tag="qsin")
                    P.st("vector", lambda e, c=cc, r=rp: e.tensor_add(
                        r[:, c * 512:(c + 1) * 512], r[:, c * 512:(c + 1) * 512],
                        xn0b[:, 0:512]),
                        reads=[kxn0b], writes=[krp_], tag="qadd")
                    if LIM <= PHASES.index("qm"):
                        continue
                    def qmf(e, c=cc, r=rp, wd=wmdup):
                        e.matmul(pqm[0:8, :], wd[0:64, :], r[0:64, c * 512:(c + 1) * 512],
                                 start=True, stop=True)
                        return e.matmul(pqm[64:72, :], wd[64:128, :],
                                        r[64:128, c * 512:(c + 1) * 512],
                                        start=True, stop=True)
                    P.st("tensor", qmf, reads=[krp_, kwmdup], writes=[kpqm], tag="qmm")
                    P.st("scalar", lambda e, c=cc, qb=qmb: e.activation(
                        qb[:, c * 512:(c + 1) * 512], pqm[:, :], AF.Sigmoid),
                        reads=[kpqm], writes=[kqmb], tag="qsig")

            if LIM <= PHASES.index("a2"):
                continue
            for cc in range(4):
                def a2f(e, c=cc):
                    e.matmul(pqm[0:64, :], wmd[0:8, :], qm2[0:8, c * 512:(c + 1) * 512],
                             start=True, stop=True)
                    return e.matmul(pqm[64:128, :], wmd[64:72, :],
                                    qm2[64:72, c * 512:(c + 1) * 512],
                                    start=True, stop=True)
                P.st("tensor", a2f, reads=[kqm, cx.t("wmd")], writes=[kpqm], tag="a2m")
                P.st("vector", lambda e, c=cc: e.tensor_copy(
                    a2b[:, c * 512:(c + 1) * 512], pqm[:, :]),
                    reads=[kpqm], writes=[ka2], tag="a2e")

            if LIM <= PHASES.index("g"):
                continue
            kpbig_sl = [kpbig.fork() for _ in range(4)]
            for e_ in range(8):
                for cc in range(4):
                    i4 = (e_ * 4 + cc)
                    pslot = i4 % 4
                    sslot = i4 % 2
                    P.st("tensor", lambda e, ee=e_, c=cc, ps2=pslot: e.matmul(
                        pbig[:, ps2 * 512:(ps2 + 1) * 512], selpq[:, ee * 128:(ee + 1) * 128],
                        a2b[:, c * 512:(c + 1) * 512], start=True, stop=True),
                        reads=[ka2, cx.t("selpq")], writes=[kpbig_sl[pslot]], tag="gpq")
                    if e_ == 0:
                        P.st("scalar", lambda e, c=cc, ps2=pslot: e.activation(
                            scr[:, c * 512:(c + 1) * 512],
                            pbig[:, ps2 * 512:(ps2 + 1) * 512], AF.Square),
                            reads=[kpbig_sl[pslot]], writes=[kscr], tag="gsq0")
                    else:
                        P.st("scalar", lambda e, ps2=pslot, s2=sslot: e.activation(
                            sqf[s2][:, :], pbig[:, ps2 * 512:(ps2 + 1) * 512], AF.Square),
                            reads=[kpbig_sl[pslot]], writes=[ksqf[sslot]], tag="gsq")
                        P.st("vector", lambda e, c=cc, s2=sslot: e.tensor_add(
                            scr[:, c * 512:(c + 1) * 512],
                            scr[:, c * 512:(c + 1) * 512], sqf[s2][:, :]),
                            reads=[ksqf[sslot]], writes=[kscr], tag="gadd")
            kpbig.absorb(kpbig_sl)
            sqt = [t1b, gtmp]
            ksqt = [kt1, kgt]
            for cc in range(4):
                P.st("scalar", lambda e, c=cc, t2=sqt[cc % 2]: e.activation(
                    t2[:, :], a2b[:, c * 512:(c + 1) * 512], AF.Square),
                    reads=[ka2], writes=[ksqt[cc % 2]], tag="a2sq")
                P.st("tensor", lambda e, c=cc, t2=sqt[cc % 2]: e.matmul(
                    pm[0][:, :], selps[:, :], t2[:, :], start=True, stop=True),
                    reads=[ksqt[cc % 2], cx.t("selps")], writes=[kpm[0]], tag="rrp")
                P.st("tensor", lambda e, c=cc, t2=sqt[cc % 2]: e.matmul(
                    pm[1][:, :], selqs[:, :], t2[:, :], start=True, stop=True),
                    reads=[ksqt[cc % 2], cx.t("selqs")], writes=[kpm[1]], tag="rrq")
                P.st("vector", lambda e, c=cc: e.tensor_sub(
                    scr[:, c * 512:(c + 1) * 512],
                    scr[:, c * 512:(c + 1) * 512], pm[0][:, :]),
                    reads=[kpm[0], kscr], writes=[kscr], tag="grps")
                P.st("vector", lambda e, c=cc: e.tensor_sub(
                    gacc[:, c * 512:(c + 1) * 512],
                    scr[:, c * 512:(c + 1) * 512], pm[1][:, :]),
                    reads=[kscr, kpm[1]], writes=[kg], tag="grqs")
            if LIM <= PHASES.index("kk"):
                continue
            for cc in range(4):
                P.st("tensor", lambda e, c=cc: e.matmul(
                    pm[0][:, :], kkp[:, :], km2[:, c * 512:(c + 1) * 512],
                    start=True, stop=True),
                    reads=[kkm, cx.t("kkp")], writes=[kpm[0]], tag="kkp_m")
                P.st("tensor", lambda e, c=cc: e.matmul(
                    pm[1][:, :], kkq[:, :], km2[:, c * 512:(c + 1) * 512],
                    start=True, stop=True),
                    reads=[kkm, cx.t("kkq")], writes=[kpm[1]], tag="kkq_m")
                P.st("vector", lambda e: e.tensor_copy(t1b[:, :], pm[0][:, :]),
                     reads=[kpm[0]], writes=[kt1], tag="kke")
                P.st("vector", lambda e, c=cc: e.tensor_tensor(
                    kk2b[:, c * 512:(c + 1) * 512], t1b[:, :], pm[1][:, :], ALU.mult),
                    reads=[kt1, kpm[1]], writes=[kkk2], tag="kkm")

            if LIM <= PHASES.index("kk"):
                continue
            for cc in range(4):
                P.st("tensor", lambda e, c=cc: e.matmul(
                    pm[1][:, :], kkq[:, :], qm2[:, c * 512:(c + 1) * 512],
                    start=True, stop=True),
                    reads=[kqm, cx.t("kkq")], writes=[kpm[1]], tag="qrep")
                P.st("vector", lambda e, c=cc: e.scalar_tensor_tensor(
                    t1b[:, :], gacc[:, c * 512:(c + 1) * 512], epsc128[:, :],
                    pm[1][:, :], ALU.add, ALU.mult),
                    reads=[kg, kpm[1], cx.t("epsc128")], writes=[kt1], tag="gqt1")
                if LIM <= PHASES.index("gqA"):
                    continue
                P.st("tensor", lambda e, c=cc: e.matmul(
                    pqm[0:40, :], grp[:, :], t1b[:, :], start=True, stop=True),
                    reads=[kt1, cx.t("grp")], writes=[kpqm], tag="gqm")
                if LIM <= PHASES.index("gqC"):
                    continue
                P.st("vector", lambda e, c=cc: e.tensor_scalar_mul(
                    gqs[:, c * 512:(c + 1) * 512], pqm[0:40, :], -2.0),
                    reads=[kpqm], writes=[kgqs], tag="gqe")
                P.st("vector", lambda e, c=cc: e.tensor_tensor(
                    t2b[0:8, :], pqm[0:8, :], qm2[0:8, c * 512:(c + 1) * 512], ALU.mult),
                    reads=[kpqm, kqm], writes=[kt2], tag="t2a")
                P.st("vector", lambda e, c=cc: e.tensor_tensor(
                    t2b[32:40, :], pqm[32:40, :], qm2[64:72, c * 512:(c + 1) * 512], ALU.mult),
                    reads=[kpqm, kqm], writes=[kt2], tag="t2b")
                if LIM <= PHASES.index("gqD"):
                    continue
                def qgqm(e, c=cc):
                    e.matmul(pm[0][0:1, :], onesc[0:8, 0:1], t2b[0:8, :],
                             start=True, stop=True)
                    return e.matmul(pm[0][64:65, :], onesc[32:40, 0:1], t2b[32:40, :],
                                    start=True, stop=True)
                P.st("tensor", qgqm, reads=[kt2, cx.t("onesc")], writes=[kpm[0]], tag="qgqm")
                if LIM <= PHASES.index("gqE"):
                    continue
                P.st("vector", lambda e, c=cc: e.tensor_copy(
                    scr[0:1, c * 512:(c + 1) * 512], pm[0][0:1, :]),
                    reads=[kpm[0]], writes=[kscr], tag="qgqeA")
                P.st("vector", lambda e, c=cc: e.tensor_copy(
                    scr[64:65, c * 512:(c + 1) * 512], pm[0][64:65, :]),
                    reads=[kpm[0]], writes=[kscr], tag="qgqeB")

            if LIM <= PHASES.index("asm"):
                continue
            for hh in range(2):
                g = p * 2 + hh
                bfi = hh
                kkhb, qghb = kkh[bfi], qgh[bfi]
                kKKb, kQGb = kKK[bfi], kQG[bfi]
                P.st("vector", lambda e, h=hh, kb=kkhb: e.tensor_copy(
                    kb[0:64, :], kk2b[h * 64:(h + 1) * 64, :]),
                    reads=[kkk2], writes=[kKKb], tag="kkcp")
                P.st("vector", lambda e, h=hh, kb=kkhb: e.tensor_copy(
                    kb[64:72, :], km2[h * 64: h * 64 + 8, :]),
                    reads=[kkm], writes=[kKKb], tag="kmcp")
                P.st("vector", lambda e, h=hh, qb=qghb: e.tensor_scalar_add(
                    qb[0:64, :], gacc[h * 64:(h + 1) * 64, :], epsc[:, :]),
                    reads=[kg, cx.t("epsc")], writes=[kQGb], tag="qgcp")
                P.st("vector", lambda e, h=hh, qb=qghb: e.tensor_copy(
                    qb[64:72, :], gqs[h * 32: h * 32 + 8, :]),
                    reads=[kgqs], writes=[kQGb], tag="gqcp")
                P.st("gpsimd", lambda e, h=hh, qb=qghb: e.dma_start(
                    out=qb[96:97, :], in_=scr[h * 64: h * 64 + 1, :]),
                    reads=[kscr], writes=[kQGb], dma=True, tag="qgqrow")

                if LIM <= PHASES.index("sc"):
                    continue
                pend_av = None
                for j in range(NT):
                    for tc in range(j // 4, 4):
                        pt, kpt = pm[tc % 2], kpm[tc % 2]
                        prb, kprb = prT[tc % 2], kpr[tc % 2]
                        P.st("tensor", lambda e, jj=j, t=tc, pt2=pt, kb=kkhb,
                             qb=qghb: e.matmul(
                            pt2[:, :], kb[0:97, jj * 128:(jj + 1) * 128],
                            qb[0:97, t * 512:(t + 1) * 512], start=True, stop=True),
                            reads=[kKKb, kQGb], writes=[kpt], tag="sc")
                        if tc == j // 4:
                            P.st("vector", lambda e, jj=j, pt2=pt: e.tensor_add(
                                pt2[:, :], pt2[:, :],
                                maskb[:, (jj % 4) * 512:((jj % 4) + 1) * 512]),
                                reads=[kmask], writes=[kpt], tag="msk")
                        if LIM <= PHASES.index("exp"):
                            continue
                        P.st("scalar", lambda e, t=tc, pt2=pt, pb=prb: e.activation(
                            pb[:, t * 512:(t + 1) * 512], pt2[:, :], AF.Exp,
                            scale=nit[:, :]),
                            reads=[kpt, knit], writes=[kprb], tag="exp")
                        P.st("vector", lambda e, t=tc, pb=prb: e.tensor_scalar_min(
                            pb[:, t * 512:(t + 1) * 512],
                            pb[:, t * 512:(t + 1) * 512], 1.0),
                            reads=[kprb], writes=[kprb], tag="min")
                        if LIM <= PHASES.index("av"):
                            continue
                        if pend_av is not None:
                            P.st("tensor", pend_av[0], reads=pend_av[1],
                                 writes=[kpbig], tag="av")
                        def avf(e, jj=j, t=tc, g2=g, pb=prb):
                            return e.matmul(
                                pbig[0:65, t * 512:(t + 1) * 512],
                                v_aug[:, jj * 1040 + g2 * 65: jj * 1040 + (g2 + 1) * 65],
                                pb[:, t * 512:(t + 1) * 512],
                                start=(jj == 0), stop=(jj == min(NT - 1, 4 * t + 3)))
                        pend_av = (avf, [kvaug, kprb])
                if pend_av is not None and LIM > PHASES.index("av"):
                    P.st("tensor", pend_av[0], reads=pend_av[1], writes=[kpbig], tag="av")

                if LIM <= PHASES.index("avn"):
                    continue
                P.st("scalar", lambda e: e.activation(scr[32:33, :], pbig[64:65, :], AF.Ln),
                     reads=[kpbig], writes=[kscr], hazard=True, tag="avln")
                P.st("scalar", lambda e: e.activation(prT[0][0:1, :], scr[32:33, :], AF.Exp, scale=-1.0),
                     reads=[kscr], writes=[kpr[0]], hazard=True, tag="avexp")
                for cc in range(4):
                    P.st("tensor", lambda e, c=cc: e.matmul(
                        pm[0][0:64, :], onesr[0:1, 0:64],
                        prT[0][0:1, c * 512:(c + 1) * 512], start=True, stop=True),
                        reads=[kpr[0], cx.t("onesr")], writes=[kpm[0]], tag="avrepl")
                    P.st("vector", lambda e: e.tensor_copy(t1b[0:64, :], pm[0][0:64, :]),
                         reads=[kpm[0]], writes=[kt1], tag="avre")
                    P.st("vector", lambda e, c=cc, h=hh, pp=p: e.tensor_tensor(
                        avT[h * 64:(h + 1) * 64, pp * T + c * 512: pp * T + (c + 1) * 512],
                        pbig[0:64, c * 512:(c + 1) * 512], t1b[0:64, :], ALU.mult),
                        reads=[kpbig, kt1], writes=[kavT], tag="avmul")

        if LIM <= PHASES.index("wo"):
            continue
        P.st("gpsimd", lambda e: e.dma_start(
            out=wvb[:, :].rearrange("p (k c) -> p k c", k=ND),
            in_=wo.rearrange("(k p) c -> p k c", p=128)),
            writes=[kwvb], dma=True, tag=f"wold{b}")
        for st_ in range(NT):
            for cc2 in range(2):
                pt, kpt = pm[cc2 % 2], kpm[cc2 % 2]
                def wop(e, s=st_, c2=cc2, pt2=pt):
                    for kd in range(ND):
                        i = e.matmul(pt2[:, :],
                                     avT[:, kd * T + s * 128: kd * T + s * 128 + 128],
                                     wvb[:, kd * D + c2 * 512: kd * D + (c2 + 1) * 512],
                                     start=(kd == 0), stop=(kd == ND - 1))
                    return i
                P.st("tensor", wop, reads=[kwvb, kavT], writes=[kpt], tag="wo")
                P.st("vector", lambda e, c2=cc2, pt2=pt: e.tensor_copy(
                    scr[:, c2 * 512:(c2 + 1) * 512], pt2[:, :]),
                    reads=[kpt], writes=[kscr], tag="woe")
            P.st("sync", lambda e, s=st_, b2=b: e.dma_start(
                out=attn0T_out[b2 * T + s * 128: b2 * T + (s + 1) * 128, :],
                in_=scr[:, 0:D]),
                reads=[kscr], dma=True, tag=f"woout{b}_{st_}")

    return None



def build_launch_A(n_batches=BL, n_pairs=H // 2):
    cx = Ctx("launchA")
    caps = declare_consts(cx, host_constants())
    x_sh = cx.dram_in("x_sh", [BL * T, D], F32)
    wq = cx.dram_in("wq", [D, D], F32)
    wk = cx.dram_in("wk", [D, D], F32)
    wv = cx.dram_in("wv", [D, D], F32)
    wo = cx.dram_in("wo", [D, D], F32)
    wqm = cx.dram_in("wqm", [64, 8], F32)
    wkm = cx.dram_in("wkm", [64, 8], F32)
    wmetric = cx.dram_in("wmetric", [8, 64], F32)
    temp = cx.dram_in("temp", [1, 1], F32)
    attn0T = cx.dram_out("attn0T", [BL * T, D], F32)
    emit_l0_attn(cx, caps, x_sh, wq, wk, wv, wo, wqm, wkm, wmetric, temp, attn0T,
                 n_batches=n_batches, n_pairs=n_pairs)
    cx.finish()
    return cx.nc


def run_launch_A(inputs):
    consts = host_constants()
    nc = build_launch_A()
    xs = np.asarray(inputs["L0_attn_x"], np.float32).reshape(B, T, D)
    base = const_inmap(consts)
    base.update({
        "wq": np.asarray(inputs["L0_attn_Wq"], np.float32),
        "wk": np.asarray(inputs["L0_attn_Wk"], np.float32),
        "wv": np.asarray(inputs["L0_attn_Wv"], np.float32),
        "wo": np.asarray(inputs["L0_attn_Wo"], np.float32),
        "wqm": np.asarray(inputs["L0_attn_Wqm"], np.float32),
        "wkm": np.asarray(inputs["L0_attn_Wkm"], np.float32),
        "wmetric": np.asarray(inputs["L0_attn_W_metric"], np.float32).reshape(8, 64),
        "temp": np.asarray(inputs["L0_attn_temperature"], np.float32).reshape(1, 1),
    })
    in_maps = []
    for c in range(NCORES):
        m = dict(base)
        m["x_sh"] = np.ascontiguousarray(xs[c * BL:(c + 1) * BL].reshape(BL * T, D))
        in_maps.append(m)
    res = run_bass_kernel_spmd(nc, in_maps, core_ids=list(range(NCORES)))
    out = np.stack([r["attn0T"].reshape(BL, T, D) for r in res.results], 0)
    return out.reshape(B, T, D)






NSLOT = 4096
NNT = NSLOT // 128
HID = 4096
NGRP = 16


def emit_l0_ffn(cx, caps, x_sh, attn0, gamma_x, gamma_slot, slot_embeds, w1, w2,
                x1_out, n_batches=BL, lim_groups=NGRP, lim_st=NT, lim_ht=32):
    nc, P = cx.nc, cx.P
    sb, ps = cx.sb, cx.ps

    eye_b = sb("eye_b", [128, 128], BF16)
    onesr = sb("onesr", [1, 128], BF16)
    seT = sb("seT", [128, ND * NSLOT], BF16)
    arena = sb("arena", [128, NGRP * 2048], BF16)
    xnT2 = sb("xnT2", [128, ND * T], BF16)
    dispS = sb("dispS", [128, 16 * 256], BF16)
    ctt = sb("ctt", [128, NNT * 128], BF16)
    xnf = [sb("xnf0", [128, D], BF16), sb("xnf1", [128, D], BF16),
           sb("xnf2", [128, D], BF16)]
    w1c = [sb(f"w1c{i6}", [128, 512], BF16) for i6 in range(6)]
    w2c = [sb(f"w2c{i6}", [128, 512], BF16) for i6 in range(6)]
    hb = sb("hb", [128, 256], BF16)
    hb2 = sb("hb2", [128, 256], BF16)
    scr = sb("scr", [128, D], F32)
    gxr = sb("gxr", [128, D], BF16)
    gsc = sb("gsc", [128, 8], F32)
    xt = sb("xt", [128, D], F32)
    mx8 = sb("mx8", [128, 8], F32)
    at_ = xt
    x1t = scr
    C = {}
    for nm in ["c_sum", "c_m", "c_v", "c_u", "c_w", "c_rstd", "c_sf", "c_a",
               "c_b", "c_sq", "c_t", "c_mx", "c_ds", "c_M", "c_S", "c_r", "c_sc"]:
        C[nm] = sb(nm, [128, 1], F32)

    pb1 = ps("pb1", [128, 2048], F32)
    pb2 = ps("pb2", [128, 1024], F32)
    pb3 = ps("pb3", [128, 512], F32)
    ptb = ps("ptb", [128, 1024], BF16)

    names = ["eye_b", "onesr", "seT", "xnT2", "dispS", "ctt", "hb", "hb2",
             "scr", "gxr", "gsc", "xt", "mx8",
             "pb1", "pb2", "pb3", "ptb",
             "xnf0", "xnf1", "xnf2"]
    names += [f"w1c{i6}" for i6 in range(6)] + [f"w2c{i6}" for i6 in range(6)]
    k = {nm: cx.t(nm) for nm in names}
    kw1c = [k[f"w1c{i6}"] for i6 in range(6)]
    kw2c = [k[f"w2c{i6}"] for i6 in range(6)]
    kC = {nm: cx.t(nm) for nm in C}
    karena = [Trk() for _ in range(NGRP)]

    xnf_sp = nc.dram_tensor("xnf_spill", [n_batches * T, D], BF16).ap()
    w1bf = nc.dram_tensor("w1bf", [D, HID], BF16).ap()
    w2bf = nc.dram_tensor("w2bf", [HID, D], BF16).ap()
    kw1bf, kw2bf = Trk(), Trk()

    P.st("sync", lambda e: e.dma_start(out=eye_b[:, :], in_=caps["eye_bf"]),
         writes=[k["eye_b"]], dma=True, tag="ldeye")
    P.st("sync", lambda e: e.dma_start(out=onesr[:, :], in_=caps["ones_row_bf"]),
         writes=[k["onesr"]], dma=True, tag="ldones")
    for i4 in range(4):
        P.st("gpsimd", lambda e, i5=i4: e.dma_start(
            out=w1bf[i5 * 256:(i5 + 1) * 256, :], in_=w1[i5 * 256:(i5 + 1) * 256, :]),
            writes=[kw1bf], dma=True, tag="w1cache")
        P.st("gpsimd", lambda e, i5=i4: e.dma_start(
            out=w2bf[i5 * 1024:(i5 + 1) * 1024, :], in_=w2[i5 * 1024:(i5 + 1) * 1024, :]),
            writes=[kw2bf], dma=True, tag="w2cache")
    P.st("gpsimd", lambda e: e.dma_start(out=xnf[0][0:1, :], in_=gamma_x),
         writes=[k["xnf0"]], dma=True, tag="ldgx")
    def gxrep(e):
        e.matmul(pb1[:, 0:512], onesr[0:1, :], xnf[0][0:1, 0:512], start=True, stop=True)
        return e.matmul(pb1[:, 512:1024], onesr[0:1, :], xnf[0][0:1, 512:1024],
                        start=True, stop=True)
    P.st("tensor", gxrep, reads=[k["onesr"], k["xnf0"]], writes=[k["pb1"]], tag="gxrep")
    P.st("vector", lambda e: e.tensor_copy(gxr[:, :], pb1[:, 0:1024]),
         reads=[k["pb1"]], writes=[k["gxr"]], tag="gxev")
    arz = P.st("vector", lambda e: e.memset(arena[:, :], 0.0), tag="arz")
    for kk2 in karena:
        kk2.did_write(arz)
    P.st("sync", lambda e: e.dma_start(
        out=scr[0:8, 0:128], in_=gamma_slot.rearrange("o (kk p) -> (o kk) p", kk=8)),
        writes=[k["scr"]], dma=True, tag="ldgs")
    P.st("vector", lambda e: e.tensor_copy(ctt[0:8, 0:128], scr[0:8, 0:128]),
         reads=[k["scr"]], writes=[k["ctt"]], tag="gscast")
    P.st("tensor", lambda e: e.transpose(ptb[:, 0:8], ctt[0:8, 0:128], eye_b[0:8, 0:8]),
         reads=[k["ctt"], k["eye_b"]], writes=[k["ptb"]], tag="gstp")
    P.st("vector", lambda e: e.tensor_scalar_mul(gsc[:, :], ptb[:, 0:8], float(np.sqrt(D))),
         reads=[k["ptb"]], writes=[k["gsc"]], tag="gsev")

    for b in range(n_batches):
        xoff = b * T

        for st_ in range(NT):
            P.st("sync", lambda e, s=st_, xo=xoff: e.dma_start(
                out=xt[:, :], in_=x_sh[xo + s * 128: xo + (s + 1) * 128, :]),
                writes=[k["xt"]], dma=True, tag="bxld")
            P.st("vector", lambda e: e.reduce_sum(C["c_sum"][:, :], xt[:, :], AX.X),
                 reads=[k["xt"]], writes=[kC["c_sum"]], hazard=True, tag="bsum")
            P.st("scalar", lambda e: e.activation(scr[:, :], xt[:, :], AF.Square,
                                                  accum_out=C["c_sq"][:, :]),
                 reads=[k["xt"]], writes=[k["scr"], kC["c_sq"]], hazard=True, tag="bsq")
            P.st("vector", lambda e: e.tensor_scalar_mul(C["c_m"][:, :], C["c_sum"][:, :], 1.0 / D),
                 reads=[kC["c_sum"]], writes=[kC["c_m"]], hazard=True, tag="bm")
            P.st("vector", lambda e: e.tensor_scalar_mul(C["c_v"][:, :], C["c_sq"][:, :], 1.0 / D),
                 reads=[kC["c_sq"]], writes=[kC["c_v"]], hazard=True, tag="bv1")
            P.st("vector", lambda e: e.tensor_mul(C["c_t"][:, :], C["c_m"][:, :], C["c_m"][:, :]),
                 reads=[kC["c_m"]], writes=[kC["c_t"]], hazard=True, tag="bm2")
            P.st("vector", lambda e: e.tensor_sub(C["c_v"][:, :], C["c_v"][:, :], C["c_t"][:, :]),
                 reads=[kC["c_t"], kC["c_v"]], writes=[kC["c_v"]], hazard=True, tag="bv2")
            P.st("scalar", lambda e: e.activation(C["c_w"][:, :], C["c_v"][:, :], AF.Ln),
                 reads=[kC["c_v"]], writes=[kC["c_w"]], hazard=True, tag="blnv")
            P.st("vector", lambda e: e.tensor_scalar_add(C["c_v"][:, :], C["c_v"][:, :], EPS_LN),
                 reads=[kC["c_v"]], writes=[kC["c_v"]], hazard=True, tag="bv3")
            P.st("scalar", lambda e: e.activation(C["c_u"][:, :], C["c_v"][:, :], AF.Ln),
                 reads=[kC["c_v"]], writes=[kC["c_u"]], hazard=True, tag="blnu")
            P.st("scalar", lambda e: e.activation(C["c_rstd"][:, :], C["c_u"][:, :], AF.Exp, scale=-0.5),
                 reads=[kC["c_u"]], writes=[kC["c_rstd"]], hazard=True, tag="brstd")
            P.st("vector", lambda e: e.tensor_sub(C["c_t"][:, :], C["c_u"][:, :], C["c_w"][:, :]),
                 reads=[kC["c_u"], kC["c_w"]], writes=[kC["c_t"]], hazard=True, tag="bsf1")
            P.st("scalar", lambda e: e.activation(C["c_sf"][:, :], C["c_t"][:, :], AF.Exp, scale=0.5),
                 reads=[kC["c_t"]], writes=[kC["c_sf"]], hazard=True, tag="bsf2")
            P.st("vector", lambda e: e.tensor_mul(C["c_a"][:, :], C["c_rstd"][:, :], C["c_sf"][:, :]),
                 reads=[kC["c_rstd"], kC["c_sf"]], writes=[kC["c_a"]], hazard=True, tag="ba")
            P.st("vector", lambda e: e.tensor_mul(C["c_b"][:, :], C["c_m"][:, :], C["c_a"][:, :]),
                 reads=[kC["c_m"], kC["c_a"]], writes=[kC["c_b"]], hazard=True, tag="bb")
            P.st("vector", lambda e: e.tensor_scalar_mul(C["c_b"][:, :], C["c_b"][:, :], -1.0),
                 reads=[kC["c_b"]], writes=[kC["c_b"]], hazard=True, tag="bnb")
            P.st("scalar", lambda e: e.activation(scr[:, :], xt[:, :], AF.Identity,
                                                  bias=C["c_b"][:, :], scale=C["c_a"][:, :]),
                 reads=[k["xt"], kC["c_a"], kC["c_b"]], writes=[k["scr"]], tag="bxa")
            xb_ = xnf[st_ % 2]
            kxb_ = k["xnf0"] if st_ % 2 == 0 else k["xnf1"]
            P.st("vector", lambda e, xb3=xb_: e.tensor_mul(xb3[:, :], scr[:, :], gxr[:, :]),
                 reads=[k["scr"], k["gxr"]], writes=[kxb_], tag="bxnf")
            P.st("gpsimd", lambda e, s=st_, b2=b, xb3=xb_: e.dma_start(
                out=xnf_sp[b2 * T + s * 128: b2 * T + (s + 1) * 128, :], in_=xb3[:, :]),
                reads=[kxb_], dma=True, tag="bspill")
            for kd in range(ND):
                P.st("tensor", lambda e, kd2=kd, xb3=xb_: e.transpose(
                    ptb[:, kd2 * 128:(kd2 + 1) * 128], xb3[:, kd2 * 128:(kd2 + 1) * 128],
                    eye_b[:, :]),
                    reads=[kxb_, k["eye_b"]], writes=[k["ptb"]], tag="btp")
            for kd in range(ND):
                P.st("vector", lambda e, kd2=kd, s=st_: e.tensor_scalar_mul(
                    xnT2[:, kd2 * T + s * 128: kd2 * T + (s + 1) * 128],
                    ptb[:, kd2 * 128:(kd2 + 1) * 128], gsc[:, kd2:kd2 + 1]),
                    reads=[k["ptb"], k["gsc"]], writes=[k["xnT2"]], tag="btc")

        if b == 0:
            for nt in range(NNT):
                P.st("sync", lambda e, n=nt: e.dma_start(
                    out=scr[:, :], in_=slot_embeds[n * 128:(n + 1) * 128, :]),
                    writes=[k["scr"]], dma=True, tag="seld")
                P.st("scalar", lambda e: e.activation(ctt[:, 0:D], scr[:, :], AF.Square,
                                                      accum_out=C["c_sq"][:, :]),
                     reads=[k["scr"]], writes=[k["ctt"], kC["c_sq"]], hazard=True, tag="sesq")
                P.st("scalar", lambda e: e.activation(C["c_u"][:, :], C["c_sq"][:, :], AF.Ln),
                     reads=[kC["c_sq"]], writes=[kC["c_u"]], hazard=True, tag="seln")
                P.st("scalar", lambda e: e.activation(C["c_rstd"][:, :], C["c_u"][:, :], AF.Exp, scale=-0.5),
                     reads=[kC["c_u"]], writes=[kC["c_rstd"]], hazard=True, tag="sers")
                P.st("scalar", lambda e, xb3=xnf[0]: e.activation(
                    xb3[:, :], scr[:, :], AF.Identity, scale=C["c_rstd"][:, :]),
                     reads=[k["scr"], kC["c_rstd"]], writes=[k["xnf0"]], tag="senrm")
                for kd in range(ND):
                    P.st("tensor", lambda e, kd2=kd, xb3=xnf[0]: e.transpose(
                        ptb[:, kd2 * 128:(kd2 + 1) * 128], xb3[:, kd2 * 128:(kd2 + 1) * 128],
                        eye_b[:, :]),
                        reads=[k["xnf0"], k["eye_b"]], writes=[k["ptb"]], tag="setp")
                P.st("vector", lambda e, n=nt: e.tensor_copy(
                    seT[:, :].rearrange("p (kk n) -> p kk n", kk=ND)[:, :, n * 128:(n + 1) * 128],
                    ptb[:, :].rearrange("p (kk n) -> p kk n", kk=ND)),
                    reads=[k["ptb"]], writes=[k["seT"]], tag="setc")

        def emit_f1_nt(g_, nt2):
            nt = g_ * 2 + nt2
            dT, kdT = ctt[:, 2048:2048 + T], k["ctt"]
            for ch in range(4):
                def f1lc(e, n=nt, c2=ch):
                    for kd in range(ND):
                        i = e.matmul(pb2[:, 0:512],
                                     seT[:, kd * NSLOT + n * 128: kd * NSLOT + (n + 1) * 128],
                                     xnT2[:, kd * T + c2 * 512: kd * T + (c2 + 1) * 512],
                                     start=(kd == 0), stop=(kd == ND - 1))
                    return i
                P.st("tensor", f1lc, reads=[k["seT"], k["xnT2"]], writes=[k["pb2"]], tag="f1l")
                P.st("vector", lambda e, c2=ch: e.tensor_copy(
                    ctt[:, c2 * 512:(c2 + 1) * 512], pb2[:, 0:512]),
                    reads=[k["pb2"]], writes=[k["ctt"]], tag="f1cp")
            P.st("vector", lambda e: e.reduce_max(C["c_mx"][:, :], ctt[:, 0:T], AX.X, negate=True),
                 reads=[k["ctt"]], writes=[kC["c_mx"]], hazard=True, tag="f1mx")
            P.st("scalar", lambda e, dT2=dT: e.activation(dT2[:, :], ctt[:, 0:T], AF.Exp,
                                                  bias=C["c_mx"][:, :],
                                                  accum_out=C["c_ds"][:, :]),
                 reads=[k["ctt"], kC["c_mx"]], writes=[kdT, kC["c_ds"]],
                 hazard=True, tag="f1e")
            P.st("vector", lambda e: e.reciprocal(C["c_t"][:, :], C["c_ds"][:, :]),
                 reads=[kC["c_ds"]], writes=[kC["c_t"]], hazard=True, tag="f1r")
            P.st("vector", lambda e, dT2=dT: e.tensor_scalar_mul(dT2[:, :], dT2[:, :], C["c_t"][:, :]),
                 reads=[kC["c_t"]], writes=[kdT], tag="f1n")
            def do_tp(n2=nt2, dT2=dT, kdT2=kdT):
                for half in range(2):
                    for jj in range(8):
                        P.st("tensor", lambda e, h2=half, j2=jj, dT3=dT2: e.transpose(
                            ptb[:, j2 * 128:(j2 + 1) * 128],
                            dT3[:, (h2 * 8 + j2) * 128:(h2 * 8 + j2 + 1) * 128],
                            eye_b[:, :]),
                            reads=[kdT2, k["eye_b"]], writes=[k["ptb"]], tag="f1tp")
                    P.st("vector", lambda e, h2=half, n3=n2: e.tensor_copy(
                        dispS[:, :].rearrange("p (s c) -> p s c", s=16)[:, h2 * 8:(h2 + 1) * 8,
                                                                       n3 * 128:(n3 + 1) * 128],
                        ptb[:, :].rearrange("p (j c) -> p j c", j=8)),
                        reads=[k["ptb"]], writes=[k["dispS"]], tag="f1tc")
            return do_tp

        def emit_slots(g_):
            kar = karena[g_]
            for st_ in range(NT):
                xb_ = xnf[st_ % 3]
                kxb_ = k[f"xnf{st_ % 3}"]
                P.st("sync" if st_ % 2 == 0 else "gpsimd",
                     lambda e, s=st_, b2=b, xb3=xb_: e.dma_start(
                         out=xb3[:, :], in_=xnf_sp[b2 * T + s * 128: b2 * T + (s + 1) * 128, :]),
                     writes=[kxb_], dma=True, tag="slld")
                def slm(e, s=st_, xb3=xb_):
                    for dd in range(ND):
                        i = e.matmul(pb1[:, dd * 256:(dd + 1) * 256],
                                     xb3[:, dd * 128:(dd + 1) * 128],
                                     dispS[:, s * 256:(s + 1) * 256],
                                     start=(s == 0 and dd % 2 == 0),
                                     stop=(s == NT - 1 and dd % 2 == 1))
                    return i
                P.st("tensor", slm, reads=[kxb_, k["dispS"]], writes=[k["pb1"]], tag="slm")
            for dd in range(ND):
                P.st("vector", lambda e, dd2=dd, g2=g_: e.tensor_copy(
                    arena[:, g2 * 2048 + dd2 * 256: g2 * 2048 + (dd2 + 1) * 256],
                    pb1[:, dd2 * 256:(dd2 + 1) * 256]),
                    reads=[k["pb1"]], writes=[kar], tag="slev")

        def emit_mlp(g_, mid):
            kar = karena[g_]
            hps = [(pb3, k["pb3"], 0), (pb2, k["pb2"], 512)]
            hbs = [(hb, k["hb"]), (hb2, k["hb2"])]
            pend = None
            for ht in range(lim_ht):
                if ht in mid:
                    mid[ht]()
                hp, khp, hoff = hps[ht % 2]
                hbb, khb = hbs[ht % 2]
                s0, s1 = (2 * ht) % 6, (2 * ht + 1) % 6
                for hf, sl in ((0, s0), (1, s1)):
                    P.st("gpsimd", lambda e, h2=ht, f2=hf, w=w1c[sl]: e.dma_start(
                        out=w[:, :].rearrange("p (kk c) -> p kk c", kk=4),
                        in_=w1bf.rearrange("(kk p) c -> p kk c", p=128)[
                            :, f2 * 4:(f2 + 1) * 4, h2 * 128:(h2 + 1) * 128]),
                        reads=[kw1bf], writes=[kw1c[sl]], dma=True, tag="w1ld")
                def hmm(e, g2=g_, wA=w1c[s0], wB=w1c[s1], hp2=hp, ho=hoff):
                    for kd in range(ND):
                        w = wA if kd < 4 else wB
                        i = e.matmul(hp2[:, ho:ho + 256],
                                     w[:, (kd % 4) * 128:(kd % 4 + 1) * 128],
                                     arena[:, g2 * 2048 + kd * 256: g2 * 2048 + (kd + 1) * 256],
                                     start=(kd == 0), stop=(kd == ND - 1))
                    return i
                P.st("tensor", hmm, reads=[kw1c[s0], kw1c[s1], kar], writes=[khp], tag="hmm")
                P.st("scalar", lambda e, hp2=hp, ho=hoff, hb3=hbb: e.activation(
                    hb3[:, :], hp2[:, ho:ho + 256], AF.Gelu),
                    reads=[khp], writes=[khb], tag="hgel")
                for ch2, sl in ((0, s0), (1, s1)):
                    P.st("sync", lambda e, h2=ht, c2=ch2, w=w2c[sl]: e.dma_start(
                        out=w[:, :], in_=w2bf[h2 * 128:(h2 + 1) * 128, c2 * 512:(c2 + 1) * 512]),
                        reads=[kw2bf], writes=[kw2c[sl]], dma=True, tag="w2ld")
                def pmm(e, h2=ht, wA=w2c[s0], wB=w2c[s1], hb3=hbb, last=(ht == lim_ht - 1)):
                    for sub in range(2):
                        for ch in range(2):
                            w = wA if ch == 0 else wB
                            i = e.matmul(pb1[:, sub * 1024 + ch * 512: sub * 1024 + (ch + 1) * 512],
                                         hb3[:, sub * 128:(sub + 1) * 128],
                                         w[:, 0:512],
                                         start=(h2 == 0), stop=last)
                    return i
                if pend is not None:
                    P.st("tensor", pend[0], reads=pend[1], writes=[k["pb1"]], tag="pmm")
                pend = (pmm, [kw2c[s0], kw2c[s1], khb])
            if pend is not None:
                P.st("tensor", pend[0], reads=pend[1], writes=[k["pb1"]], tag="pmm")
            for sub in range(2):
                P.st("vector", lambda e, s2=sub, g2=g_: e.tensor_copy(
                    arena[:, g2 * 2048 + s2 * 1024: g2 * 2048 + (s2 + 1) * 1024],
                    pb1[:, s2 * 1024:(s2 + 1) * 1024]),
                    reads=[k["pb1"]], writes=[kar], tag="pev")

        next_f1_done = False
        for g_ in range(lim_groups):
            if not next_f1_done:
                tp0 = emit_f1_nt(g_, 0)
                tp0()
                tp1 = emit_f1_nt(g_, 1)
                tp1()
            emit_slots(g_)
            mid = {}
            next_f1_done = False
            if g_ + 1 < lim_groups and lim_ht >= 17:
                nxt = g_ + 1
                tps = {}
                def m0(n2=nxt, tps2=tps):
                    tps2["tp0"] = emit_f1_nt(n2, 0)
                def m1(n2=nxt, tps2=tps):
                    tps2["tp0"]()
                    tps2["tp1"] = emit_f1_nt(n2, 1)
                def m2(tps2=tps):
                    tps2["tp1"]()
                mid = {4: m0, 10: m1, 16: m2}
                next_f1_done = True
            emit_mlp(g_, mid)

        for st_ in range(lim_st):
            kpb1_sl = [k["pb1"].fork() for _ in range(4)]
            lslots = [(pb3, k["pb3"], 0)] + [(pb1, kpb1_sl[i5], i5 * 512) for i5 in range(4)]
            for ch in range(8):
                lp, klp, loff = lslots[ch % 5]
                def lga(e, s=st_, c2=ch, lp2=lp, lo=loff):
                    for kd in range(ND):
                        i = e.matmul(lp2[:, lo:lo + 512],
                                     xnT2[:, kd * T + s * 128: kd * T + (s + 1) * 128],
                                     seT[:, kd * NSLOT + c2 * 512: kd * NSLOT + (c2 + 1) * 512],
                                     start=(kd == 0), stop=(kd == ND - 1))
                    return i
                P.st("tensor", lga, reads=[k["seT"], k["xnT2"]], writes=[klp], tag="f2al")
                P.st("vector", lambda e, c2=ch, lp2=lp, lo=loff: e.tensor_copy(
                    ctt[:, c2 * 512:(c2 + 1) * 512], lp2[:, lo:lo + 512]),
                    reads=[klp], writes=[k["ctt"]], tag="f2cp")
                P.st("vector", lambda e, c2=ch, lp2=lp, lo=loff: e.reduce_max(
                    mx8[:, c2:c2 + 1], lp2[:, lo:lo + 512], AX.X),
                    reads=[klp], writes=[k["mx8"]], hazard=True, tag="f2amx")
            k["pb1"].absorb(kpb1_sl)
            P.st("vector", lambda e: e.reduce_max(C["c_M"][:, :], mx8[:, :], AX.X),
                 reads=[k["mx8"]], writes=[kC["c_M"]], hazard=True, tag="f2aM")
            P.st("vector", lambda e: e.tensor_scalar_mul(C["c_sc"][:, :], C["c_M"][:, :], -1.0),
                 reads=[kC["c_M"]], writes=[kC["c_sc"]], hazard=True, tag="f2an")
            P.st("scalar", lambda e: e.activation(
                dispS[:, 0:2048], ctt[:, 0:2048], AF.Exp, bias=C["c_sc"][:, :],
                accum_out=C["c_S"][:, :]),
                reads=[k["ctt"], kC["c_sc"]], writes=[k["dispS"], kC["c_S"]],
                hazard=True, tag="f2e1")
            P.st("scalar", lambda e: e.activation(
                dispS[:, 2048:4096], ctt[:, 2048:4096], AF.Exp, bias=C["c_sc"][:, :],
                accum_out=C["c_ds"][:, :]),
                reads=[k["ctt"], kC["c_sc"]], writes=[k["dispS"], kC["c_ds"]],
                hazard=True, tag="f2e2")
            P.st("vector", lambda e: e.tensor_add(C["c_t"][:, :], C["c_S"][:, :], C["c_ds"][:, :]),
                 reads=[kC["c_S"], kC["c_ds"]], writes=[kC["c_t"]], hazard=True, tag="f2Ss")
            P.st("vector", lambda e: e.reciprocal(C["c_r"][:, :], C["c_t"][:, :]),
                 reads=[kC["c_t"]], writes=[kC["c_r"]], hazard=True, tag="f2rc")
            P.st("vector", lambda e: e.tensor_scalar_mul(
                dispS[:, 0:2048], dispS[:, 0:2048], C["c_r"][:, :]),
                reads=[kC["c_r"]], writes=[k["dispS"]], tag="f2sc1")
            P.st("vector", lambda e: e.tensor_scalar_mul(
                dispS[:, 2048:4096], dispS[:, 2048:4096], C["c_r"][:, :]),
                reads=[kC["c_r"]], writes=[k["dispS"]], tag="f2sc2")
            for q4 in range(4):
                for jj in range(8):
                    P.st("tensor", lambda e, q2=q4, j2=jj: e.transpose(
                        ptb[:, j2 * 128:(j2 + 1) * 128],
                        dispS[:, (q2 * 8 + j2) * 128:(q2 * 8 + j2 + 1) * 128],
                        eye_b[:, :]),
                        reads=[k["dispS"], k["eye_b"]], writes=[k["ptb"]], tag="f2tp")
                P.st("vector", lambda e, q2=q4: e.tensor_copy(
                    ctt[:, q2 * 1024:(q2 + 1) * 1024], ptb[:, :]),
                    reads=[k["ptb"]], writes=[k["ctt"]], tag="f2tc")
            for nt in range(NNT):
                g2 = nt // 2
                def omm(e, n=nt, g3=g2):
                    for ch2 in range(2):
                        i = e.matmul(pb2[:, ch2 * 512:(ch2 + 1) * 512],
                                     ctt[:, n * 128:(n + 1) * 128],
                                     arena[:, g3 * 2048 + (n % 2) * 1024 + ch2 * 512:
                                           g3 * 2048 + (n % 2) * 1024 + (ch2 + 1) * 512],
                                     start=(n == 0), stop=(n == NNT - 1))
                    return i
                P.st("tensor", omm, reads=[k["ctt"], karena[g2]],
                     writes=[k["pb2"]], tag="omm")
            P.st("sync", lambda e, s=st_, b2=b: e.dma_start(
                out=at_[:, :], in_=attn0[b2 * T + s * 128: b2 * T + (s + 1) * 128, :]),
                writes=[k["xt"]], dma=True, tag="atld")
            P.st("vector", lambda e: e.tensor_add(x1t[:, :], pb2[:, :], at_[:, :]),
                 reads=[k["pb2"], k["xt"]], writes=[k["scr"]], tag="bres1")
            P.st("sync", lambda e, s=st_, b2=b: e.dma_start(
                out=at_[:, :], in_=x_sh[b2 * T + s * 128: b2 * T + (s + 1) * 128, :]),
                writes=[k["xt"]], dma=True, tag="bxld2")
            P.st("vector", lambda e: e.tensor_add(x1t[:, :], x1t[:, :], at_[:, :]),
                 reads=[k["xt"]], writes=[k["scr"]], tag="bres2")
            P.st("sync", lambda e, s=st_, b2=b: e.dma_start(
                out=x1_out[b2 * T + s * 128: b2 * T + (s + 1) * 128, :], in_=x1t[:, :]),
                reads=[k["scr"]], dma=True, tag="x1out")


def build_launch_B(n_batches=BL, lim_groups=NGRP, lim_st=NT, lim_ht=32):
    cx = Ctx("launchB")
    caps = declare_consts(cx, host_constants())
    x_sh = cx.dram_in("x_sh", [BL * T, D], F32)
    attn0 = cx.dram_in("attn0", [BL * T, D], F32)
    gamma_x = cx.dram_in("gamma_x", [1, D], F32)
    gamma_slot = cx.dram_in("gamma_slot", [1, D], F32)
    slot_embeds = cx.dram_in("slot_embeds", [NSLOT, D], F32)
    w1 = cx.dram_in("w1", [D, HID], F32)
    w2 = cx.dram_in("w2", [HID, D], F32)
    x1 = cx.dram_out("x1", [BL * T, D], F32)
    emit_l0_ffn(cx, caps, x_sh, attn0, gamma_x, gamma_slot, slot_embeds, w1, w2,
                x1, n_batches=n_batches, lim_groups=lim_groups, lim_st=lim_st,
                lim_ht=lim_ht)
    cx.finish()
    return cx.nc



def emit_wT(cx, name, src, nrows, tmp, ktmp, ptb, kptb, eye_b, keye):
    P = cx.P
    buf = cx.sb(name, [128, ND * nrows], BF16)
    kbuf = cx.t(name)
    for rt in range(nrows // 128):
        P.st("gpsimd", lambda e, r=rt: e.dma_start(
            out=tmp[:, :], in_=src[r * 128:(r + 1) * 128, :]),
            writes=[ktmp], dma=True, tag=f"wT_{name}_ld")
        for kd in range(ND):
            P.st("tensor", lambda e, kd2=kd: e.transpose(
                ptb[:, kd2 * 128:(kd2 + 1) * 128], tmp[:, kd2 * 128:(kd2 + 1) * 128],
                eye_b[:, :]),
                reads=[ktmp, keye], writes=[kptb], tag=f"wT_{name}_tp")
        P.st("vector", lambda e, r=rt: e.tensor_copy(
            buf[:, :].rearrange("p (kk c) -> p kk c", kk=ND)[:, :, r * 128:(r + 1) * 128],
            ptb[:, :].rearrange("p (kk c) -> p kk c", kk=ND)),
            reads=[kptb], writes=[kbuf], tag=f"wT_{name}_ev")
    return buf, kbuf


def emit_cols(cx, name, src, tmp8, ktmp8, ptb, kptb, eye_b, keye, scale=1.0):
    P = cx.P
    buf = cx.sb(name, [128, 8], F32)
    kbuf = cx.t(name)
    P.st("sync", lambda e: e.dma_start(
        out=tmp8[:, :], in_=src.rearrange("o (kk p) -> (o kk) p", kk=8)),
        writes=[ktmp8], dma=True, tag=f"cols_{name}_ld")
    P.st("tensor", lambda e: e.transpose(ptb[:, 0:8], tmp8[:, :], eye_b[0:8, 0:8]),
         reads=[ktmp8, keye], writes=[kptb], tag=f"cols_{name}_tp")
    P.st("vector", lambda e: e.tensor_scalar_mul(buf[:, :], ptb[:, 0:8], scale),
         reads=[kptb], writes=[kbuf], tag=f"cols_{name}_ev")
    return buf, kbuf



NKV = 4


def emit_l1_attn(cx, caps, x1_sh, wq, wk, wv, wo, attn1_out, n_batches=BL,
                 n_heads=H, xg_spill=None, add_resid=False):
    nc, P = cx.nc, cx.P
    sb, ps = cx.sb, cx.ps

    eye_b = sb("eye_b", [128, 128], BF16)
    eye_f = sb("eye_f", [128, 128], F32)
    onesr = sb("onesr", [1, 128], BF16)
    m1 = sb("m1", [128, 4 * 512], BF16)
    m2 = sb("m2", [128, 4 * 512], BF16)
    xn1T = sb("xn1T", [128, ND * T], BF16)
    qT = sb("qT", [128, ND * T], BF16)
    kTb = sb("kTb", [128, 2 * T], BF16)
    v_aug = sb("v_aug", [128, NT * (NKV * 65)], BF16)
    avT = sb("avT", [128, ND * T], BF16)
    qh = sb("qh", [128, T], BF16)
    kh = sb("kh", [128, T], BF16)
    prT = [sb("prT0", [128, T], BF16), sb("prT1", [128, T], BF16)]
    mcol = sb("mcol", [128, NT], F32)
    scr = sb("scr", [128, T], F32)
    xt = sb("xt", [128, D], F32)
    xnb = sb("xnb", [128, D], BF16)
    wtmp = sb("wtmp", [128, D], BF16)
    C = {}
    for nm in ["c_sum", "c_m", "c_v", "c_u", "c_rstd", "c_nm", "c_sq", "c_t", "c_mx"]:
        C[nm] = sb(nm, [128, 1], F32)

    pm = [ps("pm0", [128, 512], F32), ps("pm1", [128, 512], F32)]
    pqm = ps("pqm", [128, 512], F32)
    pbig = ps("pbig", [128, T], F32)
    ptb = ps("ptb", [128, 1024], BF16)

    k = {nm: cx.t(nm) for nm in ["eye_b", "onesr", "m1", "m2", "xn1T", "qT",
                                 "kTb", "v_aug", "avT", "qh", "kh", "prT0",
                                 "prT1", "mcol", "scr", "xt", "xnb", "wtmp",
                                 "pm0", "pm1", "pqm", "pbig", "ptb", "eye_f"]}
    kC = {nm: cx.t(nm) for nm in C}
    kpm = [k["pm0"], k["pm1"]]
    kpr = [k["prT0"], k["prT1"]]

    for nm, buf, tn in [("eye_bf", eye_b, "eye_b"), ("eye_f32", eye_f, "eye_f"),
                        ("ones_row_bf", onesr, "onesr")]:
        P.st("sync", lambda e, b2=buf, a=caps[nm]: e.dma_start(out=b2[:, :], in_=a),
             writes=[cx.t(tn)], dma=True, tag=f"ld_{nm}")
    P.st("sync", lambda e: e.dma_start(
        out=m1[:, :].rearrange("p (m f) -> p m f", m=4),
        in_=caps["masksP1"].rearrange("(m p) f -> p m f", m=4)),
        writes=[k["m1"]], dma=True, tag="ldm1")
    P.st("sync", lambda e: e.dma_start(
        out=m2[:, :].rearrange("p (m f) -> p m f", m=4),
        in_=caps["masksP2"].rearrange("(m p) f -> p m f", m=4)),
        writes=[k["m2"]], dma=True, tag="ldm2")

    wqT, kwqT = emit_wT(cx, "wqT", wq, D, wtmp, k["wtmp"], ptb, k["ptb"], eye_b, k["eye_b"])
    wkT, kwkT = emit_wT(cx, "wkT", wk, 256, wtmp, k["wtmp"], ptb, k["ptb"], eye_b, k["eye_b"])
    wvT, kwvT = emit_wT(cx, "wvT", wv, 256, wtmp, k["wtmp"], ptb, k["ptb"], eye_b, k["eye_b"])
    woT, kwoT = emit_wT(cx, "woT", wo, D, wtmp, k["wtmp"], ptb, k["ptb"], eye_b, k["eye_b"])

    P.st("vector", lambda e: e.memset(kh[:, :], 0.0), writes=[k["kh"]], tag="kh0")
    P.st("vector", lambda e: e.memset(qh[:, :], 0.0), writes=[k["qh"]], tag="qh0")
    P.st("vector", lambda e: e.memset(avT[:, :], 0.0), writes=[k["avT"]], tag="avT0")
    P.st("sync", lambda e: e.dma_start(out=kh[64:65, :], in_=caps["ones_2048_bf"]),
         writes=[k["kh"]], dma=True, tag="khones")
    P.st("vector", lambda e: e.memset(
        v_aug[:, :].rearrange("p (s h c) -> p s h c", s=NT, h=NKV)[:, :, :, 64:65], 1.0),
        writes=[k["v_aug"]], tag="vaug1")

    for b in range(n_batches):
        for st_ in range(NT):
            P.st("sync", lambda e, s=st_, b2=b: e.dma_start(
                out=xt[:, :], in_=x1_sh[b2 * T + s * 128: b2 * T + (s + 1) * 128, :]),
                writes=[k["xt"]], dma=True, tag="cxld")
            P.st("vector", lambda e: e.reduce_sum(C["c_sum"][:, :], xt[:, :], AX.X),
                 reads=[k["xt"]], writes=[kC["c_sum"]], hazard=True, tag="csum")
            P.st("scalar", lambda e: e.activation(scr[:, 0:D], xt[:, :], AF.Square,
                                                  accum_out=C["c_sq"][:, :]),
                 reads=[k["xt"]], writes=[k["scr"], kC["c_sq"]], hazard=True, tag="csq")
            P.st("vector", lambda e: e.tensor_scalar_mul(C["c_m"][:, :], C["c_sum"][:, :], 1.0 / D),
                 reads=[kC["c_sum"]], writes=[kC["c_m"]], hazard=True, tag="cm")
            P.st("vector", lambda e: e.tensor_scalar_mul(C["c_v"][:, :], C["c_sq"][:, :], 1.0 / D),
                 reads=[kC["c_sq"]], writes=[kC["c_v"]], hazard=True, tag="cv1")
            P.st("vector", lambda e: e.tensor_mul(C["c_t"][:, :], C["c_m"][:, :], C["c_m"][:, :]),
                 reads=[kC["c_m"]], writes=[kC["c_t"]], hazard=True, tag="cm2")
            P.st("vector", lambda e: e.tensor_sub(C["c_v"][:, :], C["c_v"][:, :], C["c_t"][:, :]),
                 reads=[kC["c_t"], kC["c_v"]], writes=[kC["c_v"]], hazard=True, tag="cv2")
            P.st("vector", lambda e: e.tensor_scalar_add(C["c_v"][:, :], C["c_v"][:, :], EPS_LN),
                 reads=[kC["c_v"]], writes=[kC["c_v"]], hazard=True, tag="cv3")
            P.st("scalar", lambda e: e.activation(C["c_u"][:, :], C["c_v"][:, :], AF.Ln),
                 reads=[kC["c_v"]], writes=[kC["c_u"]], hazard=True, tag="cln")
            P.st("scalar", lambda e: e.activation(C["c_rstd"][:, :], C["c_u"][:, :], AF.Exp, scale=-0.5),
                 reads=[kC["c_u"]], writes=[kC["c_rstd"]], hazard=True, tag="cexp")
            P.st("vector", lambda e: e.tensor_mul(C["c_nm"][:, :], C["c_m"][:, :], C["c_rstd"][:, :]),
                 reads=[kC["c_m"], kC["c_rstd"]], writes=[kC["c_nm"]], hazard=True, tag="cnm")
            P.st("vector", lambda e: e.tensor_scalar_mul(C["c_nm"][:, :], C["c_nm"][:, :], -1.0),
                 reads=[kC["c_nm"]], writes=[kC["c_nm"]], hazard=True, tag="cnm2")
            P.st("scalar", lambda e: e.activation(xnb[:, :], xt[:, :], AF.Identity,
                                                  bias=C["c_nm"][:, :], scale=C["c_rstd"][:, :]),
                 reads=[k["xt"], kC["c_nm"], kC["c_rstd"]], writes=[k["xnb"]], tag="cxn")
            for kd in range(ND):
                P.st("tensor", lambda e, kd2=kd: e.transpose(
                    ptb[:, kd2 * 128:(kd2 + 1) * 128], xnb[:, kd2 * 128:(kd2 + 1) * 128],
                    eye_b[:, :]),
                    reads=[k["xnb"], k["eye_b"]], writes=[k["ptb"]], tag="ctp")
            P.st("vector", lambda e, s=st_: e.tensor_copy(
                xn1T[:, :].rearrange("p (kk t) -> p kk t", kk=ND)[:, :, s * 128:(s + 1) * 128],
                ptb[:, :].rearrange("p (kk t) -> p kk t", kk=ND)),
                reads=[k["ptb"]], writes=[k["xn1T"]], tag="ctc")
            if xg_spill is not None:
                P.st("gpsimd", lambda e, s=st_, b2=b: e.dma_start(
                    out=xg_spill.rearrange("(kk p) c -> p kk c", p=128)[
                        :, :, b2 * T + s * 128: b2 * T + (s + 1) * 128],
                    in_=xn1T[:, :].rearrange("p (kk t) -> p kk t", kk=ND)[
                        :, :, s * 128:(s + 1) * 128]),
                    reads=[k["xn1T"]], dma=True, tag="cxgsp")

        for ct in range(ND):
            for cc in range(4):
                pt, kpt = pm[cc % 2], kpm[cc % 2]
                def qproj(e, c=ct, c2=cc, pt2=pt):
                    for kd in range(ND):
                        i = e.matmul(pt2[:, :], wqT[:, kd * D + c * 128: kd * D + c * 128 + 128],
                                     xn1T[:, kd * T + c2 * 512: kd * T + (c2 + 1) * 512],
                                     start=(kd == 0), stop=(kd == ND - 1))
                    return i
                P.st("tensor", qproj, reads=[kwqT, k["xn1T"]], writes=[kpt], tag="cqp")
                P.st("vector", lambda e, c=ct, c2=cc, pt2=pt: e.tensor_copy(
                    qT[:, c * T + c2 * 512: c * T + (c2 + 1) * 512], pt2[:, :]),
                    reads=[kpt], writes=[k["qT"]], tag="cqe")
        for ct in range(2):
            for cc in range(4):
                pt, kpt = pm[cc % 2], kpm[cc % 2]
                def kproj(e, c=ct, c2=cc, pt2=pt):
                    for kd in range(ND):
                        i = e.matmul(pt2[:, :], wkT[:, kd * 256 + c * 128: kd * 256 + c * 128 + 128],
                                     xn1T[:, kd * T + c2 * 512: kd * T + (c2 + 1) * 512],
                                     start=(kd == 0), stop=(kd == ND - 1))
                    return i
                P.st("tensor", kproj, reads=[kwkT, k["xn1T"]], writes=[kpt], tag="ckp")
                P.st("vector", lambda e, c=ct, c2=cc, pt2=pt: e.tensor_copy(
                    kTb[:, c * T + c2 * 512: c * T + (c2 + 1) * 512], pt2[:, :]),
                    reads=[kpt], writes=[k["kTb"]], tag="cke")
        for st_ in range(NT):
            pt, kpt = pm[st_ % 2], kpm[st_ % 2]
            def vproj(e, s=st_, pt2=pt):
                for kd in range(ND):
                    i = e.matmul(pt2[:, 0:256],
                                 xn1T[:, kd * T + s * 128: kd * T + (s + 1) * 128],
                                 wvT[:, kd * 256:(kd + 1) * 256],
                                 start=(kd == 0), stop=(kd == ND - 1))
                return i
            P.st("tensor", vproj, reads=[kwvT, k["xn1T"]], writes=[kpt], tag="cvp")
            P.st("vector", lambda e, s=st_, pt2=pt: e.tensor_copy(
                v_aug[:, s * 260:(s + 1) * 260].rearrange("p (h c) -> p h c", h=NKV)[:, :, 0:64],
                pt2[:, 0:256].rearrange("p (h c) -> p h c", h=NKV)),
                reads=[kpt], writes=[k["v_aug"]], tag="cve")

        for g in range(n_heads):
            kv = g // 4
            ctq, rq = g // 2, (g % 2) * 64
            ctk, rk = kv // 2, (kv % 2) * 64
            P.st("vector", lambda e, c=ctq, r=rq: e.tensor_copy(
                qh[0:64, :], qT[r:r + 64, c * T:(c + 1) * T]),
                reads=[k["qT"]], writes=[k["qh"]], tag="chq")
            if g % 4 == 0:
                P.st("vector", lambda e, c=ctk, r=rk: e.tensor_copy(
                    kh[0:64, :], kTb[r:r + 64, c * T:(c + 1) * T]),
                    reads=[k["kTb"]], writes=[k["kh"]], tag="chk")
            for j in range(NT):
                for tc in range(j // 4 + 1):
                    pt, kpt = pm[tc % 2], kpm[tc % 2]
                    P.st("tensor", lambda e, jj=j, t=tc, pt2=pt: e.matmul(
                        pt2[:, :], qh[0:64, jj * 128:(jj + 1) * 128],
                        kh[0:64, t * 512:(t + 1) * 512], start=True, stop=True),
                        reads=[k["qh"], k["kh"]], writes=[kpt], tag="cp1")
                    if tc == j // 4:
                        P.st("vector", lambda e, jj=j, pt2=pt: e.tensor_add(
                            pt2[:, :], pt2[:, :], m1[:, (jj % 4) * 512:((jj % 4) + 1) * 512]),
                            reads=[k["m1"]], writes=[kpt], tag="cmsk1")
                    if tc == 0:
                        P.st("vector", lambda e, pt2=pt: e.reduce_max(
                            C["c_mx"][:, :], pt2[:, :], AX.X),
                            reads=[kpt], writes=[kC["c_mx"]], hazard=True, tag="cmx0")
                    else:
                        P.st("vector", lambda e, pt2=pt: e.reduce_max(
                            C["c_t"][:, :], pt2[:, :], AX.X),
                            reads=[kpt], writes=[kC["c_t"]], hazard=True, tag="cmxc")
                        P.st("vector", lambda e: e.tensor_max(
                            C["c_mx"][:, :], C["c_mx"][:, :], C["c_t"][:, :]),
                            reads=[kC["c_t"]], writes=[kC["c_mx"]], hazard=True, tag="cmxm")
                P.st("vector", lambda e, jj=j: e.tensor_copy(
                    mcol[:, jj:jj + 1], C["c_mx"][:, :]),
                    reads=[kC["c_mx"]], writes=[k["mcol"]], hazard=True, tag="cmcs")
            for j in range(NT):
                P.st("tensor", lambda e, jj=j: e.transpose(
                    pbig[0:1, jj * 128:(jj + 1) * 128], mcol[:, jj:jj + 1], eye_f[:, :]),
                    reads=[k["mcol"], k["eye_f"]], writes=[k["pbig"]], tag="cmtp")
            P.st("vector", lambda e: e.tensor_scalar_mul(qh[64:65, :], pbig[0:1, :], -1.0),
                 reads=[k["pbig"]], writes=[k["qh"]], tag="cmrow")
            pend_av = None
            for j in range(NT):
                for tc in range(j // 4, 4):
                    pt, kpt = pm[tc % 2], kpm[tc % 2]
                    prb, kprb = prT[tc % 2], kpr[tc % 2]
                    P.st("tensor", lambda e, jj=j, t=tc, pt2=pt: e.matmul(
                        pt2[:, :], kh[0:65, jj * 128:(jj + 1) * 128],
                        qh[0:65, t * 512:(t + 1) * 512], start=True, stop=True),
                        reads=[k["qh"], k["kh"]], writes=[kpt], tag="cp2")
                    if tc == j // 4:
                        P.st("vector", lambda e, jj=j, pt2=pt: e.tensor_add(
                            pt2[:, :], pt2[:, :], m2[:, (jj % 4) * 512:((jj % 4) + 1) * 512]),
                            reads=[k["m2"]], writes=[kpt], tag="cmsk2")
                    P.st("scalar", lambda e, t=tc, pt2=pt, pb=prb: e.activation(
                        pb[:, t * 512:(t + 1) * 512], pt2[:, :], AF.Exp, scale=0.125),
                        reads=[kpt], writes=[kprb], tag="cexp2")
                    if pend_av is not None:
                        P.st("tensor", pend_av[0], reads=pend_av[1],
                             writes=[k["pbig"]], tag="cav")
                    def cavf(e, jj=j, t=tc, kv2=kv, pb=prb):
                        return e.matmul(
                            pbig[0:65, t * 512:(t + 1) * 512],
                            v_aug[:, jj * 260 + kv2 * 65: jj * 260 + (kv2 + 1) * 65],
                            pb[:, t * 512:(t + 1) * 512],
                            start=(jj == 0), stop=(jj == min(NT - 1, 4 * t + 3)))
                    pend_av = (cavf, [k["v_aug"], kprb])
            if pend_av is not None:
                P.st("tensor", pend_av[0], reads=pend_av[1], writes=[k["pbig"]], tag="cav")
            P.st("scalar", lambda e: e.activation(scr[32:33, :], pbig[64:65, :], AF.Ln),
                 reads=[k["pbig"]], writes=[k["scr"]], hazard=True, tag="cavln")
            P.st("scalar", lambda e: e.activation(prT[0][0:1, :], scr[32:33, :], AF.Exp, scale=-1.0),
                 reads=[k["scr"]], writes=[kpr[0]], hazard=True, tag="cavexp")
            for cc in range(4):
                P.st("tensor", lambda e, c=cc: e.matmul(
                    pm[0][0:64, :], onesr[0:1, 0:64],
                    prT[0][0:1, c * 512:(c + 1) * 512], start=True, stop=True),
                    reads=[kpr[0], k["onesr"]], writes=[kpm[0]], tag="cavr")
                P.st("vector", lambda e: e.tensor_copy(scr[0:64, 0:512], pm[0][0:64, :]),
                     reads=[kpm[0]], writes=[k["scr"]], tag="cavre")
                P.st("vector", lambda e, c=cc, g2=g: e.tensor_tensor(
                    avT[(g2 % 2) * 64:((g2 % 2) + 1) * 64,
                        (g2 // 2) * T + c * 512:(g2 // 2) * T + (c + 1) * 512],
                    pbig[0:64, c * 512:(c + 1) * 512], scr[0:64, 0:512], ALU.mult),
                    reads=[k["pbig"], k["scr"]], writes=[k["avT"]], tag="cavm")

        for st_ in range(NT):
            for cc2 in range(2):
                pt, kpt = pm[cc2 % 2], kpm[cc2 % 2]
                def wop(e, s=st_, c2=cc2, pt2=pt):
                    for kd in range(ND):
                        i = e.matmul(pt2[:, :],
                                     avT[:, kd * T + s * 128: kd * T + s * 128 + 128],
                                     woT[:, kd * D + c2 * 512: kd * D + (c2 + 1) * 512],
                                     start=(kd == 0), stop=(kd == ND - 1))
                    return i
                P.st("tensor", wop, reads=[kwoT, k["avT"]], writes=[kpt], tag="cwo")
                P.st("vector", lambda e, c2=cc2, pt2=pt: e.tensor_copy(
                    scr[:, c2 * 512:(c2 + 1) * 512], pt2[:, :]),
                    reads=[kpt], writes=[k["scr"]], tag="cwoe")
            if add_resid:
                P.st("sync", lambda e, s=st_, b2=b: e.dma_start(
                    out=xt[:, :], in_=x1_sh[b2 * T + s * 128: b2 * T + (s + 1) * 128, :]),
                    writes=[k["xt"]], dma=True, tag="cx1ld")
                P.st("vector", lambda e: e.tensor_add(scr[:, 0:D], scr[:, 0:D], xt[:, :]),
                     reads=[k["xt"]], writes=[k["scr"]], tag="cresid")
            P.st("sync", lambda e, s=st_, b2=b: e.dma_start(
                out=attn1_out[b2 * T + s * 128: b2 * T + (s + 1) * 128, :],
                in_=scr[:, 0:D]),
                reads=[k["scr"]], dma=True, tag="cwout")


def build_launch_C(n_batches=BL, n_heads=H):
    cx = Ctx("launchC")
    caps = declare_consts(cx, host_constants())
    x1_sh = cx.dram_in("x1_sh", [BL * T, D], F32)
    wq = cx.dram_in("wq", [D, D], F32)
    wk = cx.dram_in("wk", [256, D], F32)
    wv = cx.dram_in("wv", [256, D], F32)
    wo = cx.dram_in("wo", [D, D], F32)
    attn1 = cx.dram_out("attn1", [BL * T, D], F32)
    emit_l1_attn(cx, caps, x1_sh, wq, wk, wv, wo, attn1,
                 n_batches=n_batches, n_heads=n_heads)
    cx.finish()
    return cx.nc



RS = T // NCORES
NRT = B * RS // 128


def emit_l1_ffn(cx, caps, x1_r, attn1_r, wf, uf, wi, ui, wc, uc, wo, uo,
                bfv, biv, bcv, bov, incv, out_r, n_steps=B, n_rt=NRT,
                xg_in=None, s1_r=None):
    nc, P = cx.nc, cx.P
    sb, ps = cx.sb, cx.ps

    eye_b = sb("eye_b", [128, 128], BF16)
    eye_f = sb("eye_f", [128, 128], F32)
    onesr = sb("onesr", [1, 128], BF16)
    wtmp = sb("wtmp", [128, D], BF16)
    scr = sb("scr", [128, D], F32)
    xt = sb("xt", [128, D], F32)
    xnb = sb("xnb", [128, D], BF16)
    grow8 = sb("grow8", [8, 128], F32)
    cT = sb("cT", [128, ND * RS], F32)
    cTb = sb("cTb", [128, ND * RS], BF16)
    xstep = [sb("xs0", [128, ND * RS], BF16), sb("xs1", [128, ND * RS], BF16)]
    fb = sb("fb", [128, ND * RS], F32)
    ib = sb("ib", [128, ND * RS], F32)
    gb = sb("gb", [128, ND * RS], F32)
    tmb = sb("tmb", [128, RS], F32)
    boR = sb("boR", [128, D], F32)
    borow = sb("borow", [1, D], BF16)
    hx = sb("hx", [128, ND * 128], BF16)
    C = {}
    for nm in ["c_sum", "c_m", "c_v", "c_u", "c_rstd", "c_nm", "c_sq", "c_t"]:
        C[nm] = sb(nm, [128, 1], F32)

    pg = [ps("pg0", [128, 512], F32), ps("pg1", [128, 512], F32), ps("pg2", [128, 512], F32)]
    ph = ps("ph", [128, 1024], F32)
    ptb = ps("ptb", [128, 1024], BF16)
    pmA = ps("pmA", [128, 512], F32)

    k = {nm: cx.t(nm) for nm in ["eye_b", "eye_f", "onesr", "wtmp", "scr", "xt", "xnb",
                                 "grow8", "cT", "cTb", "xs0", "xs1", "fb", "ib", "gb",
                                 "tmb", "boR", "borow", "hx",
                                 "pg0", "pg1", "pg2", "ph", "ptb", "pmA"]}
    kC = {nm: cx.t(nm) for nm in C}
    kxs = [k["xs0"], k["xs1"]]

    if xg_in is None:
        xgT = nc.dram_tensor("xgT_spill", [D, B * RS], BF16).ap()
    else:
        xgT = xg_in
    kxgT = Trk()

    P.st("sync", lambda e: e.dma_start(out=eye_b[:, :], in_=caps["eye_bf"]),
         writes=[k["eye_b"]], dma=True, tag="ldeye")
    P.st("sync", lambda e: e.dma_start(out=eye_f[:, :], in_=caps["eye_f32"]),
         writes=[k["eye_f"]], dma=True, tag="ldeyef")
    P.st("sync", lambda e: e.dma_start(out=onesr[:, :], in_=caps["ones_row_bf"]),
         writes=[k["onesr"]], dma=True, tag="ldones")

    wfT, kwfT = emit_wT(cx, "wfT", wf, D, wtmp, k["wtmp"], ptb, k["ptb"], eye_b, k["eye_b"])
    ufT, kufT = emit_wT(cx, "ufT", uf, D, wtmp, k["wtmp"], ptb, k["ptb"], eye_b, k["eye_b"])
    wiT, kwiT = emit_wT(cx, "wiT", wi, D, wtmp, k["wtmp"], ptb, k["ptb"], eye_b, k["eye_b"])
    uiT, kuiT = emit_wT(cx, "uiT", ui, D, wtmp, k["wtmp"], ptb, k["ptb"], eye_b, k["eye_b"])
    wcT, kwcT = emit_wT(cx, "wcT", wc, D, wtmp, k["wtmp"], ptb, k["ptb"], eye_b, k["eye_b"])
    ucT, kucT = emit_wT(cx, "ucT", uc, D, wtmp, k["wtmp"], ptb, k["ptb"], eye_b, k["eye_b"])
    woT, kwoT = emit_wT(cx, "woT", wo, D, wtmp, k["wtmp"], ptb, k["ptb"], eye_b, k["eye_b"])
    uoT, kuoT = emit_wT(cx, "uoT", uo, D, wtmp, k["wtmp"], ptb, k["ptb"], eye_b, k["eye_b"])

    bfc, kbfc = emit_cols(cx, "bfc", bfv, grow8, k["grow8"], pmA, k["pmA"], eye_f, k["eye_f"])
    bic, kbic = emit_cols(cx, "bic", biv, grow8, k["grow8"], pmA, k["pmA"], eye_f, k["eye_f"])
    bcc, kbcc = emit_cols(cx, "bcc", bcv, grow8, k["grow8"], pmA, k["pmA"], eye_f, k["eye_f"])
    icc, kicc = emit_cols(cx, "icc", incv, grow8, k["grow8"], pmA, k["pmA"], eye_f, k["eye_f"])

    P.st("sync", lambda e: e.dma_start(out=scr[0:1, :], in_=bov),
         writes=[k["scr"]], dma=True, tag="ldbo")
    P.st("vector", lambda e: e.tensor_copy(borow[0:1, :], scr[0:1, :]),
         reads=[k["scr"]], writes=[k["borow"]], tag="bocast")
    def borep(e):
        e.matmul(pmA[:, :], onesr[0:1, :], borow[0:1, 0:512], start=True, stop=True)
        return e.matmul(ph[:, 0:512], onesr[0:1, :], borow[0:1, 512:1024],
                        start=True, stop=True)
    P.st("tensor", borep, reads=[k["onesr"], k["borow"]], writes=[k["pmA"], k["ph"]], tag="borep")
    P.st("vector", lambda e: e.tensor_copy(boR[:, 0:512], pmA[:, :]),
         reads=[k["pmA"]], writes=[k["boR"]], tag="boe1")
    P.st("vector", lambda e: e.tensor_copy(boR[:, 512:1024], ph[:, 0:512]),
         reads=[k["ph"]], writes=[k["boR"]], tag="boe2")

    for rt in (range(n_rt) if xg_in is None else []):
        P.st("sync", lambda e, r=rt: e.dma_start(
            out=xt[:, :], in_=x1_r[r * 128:(r + 1) * 128, :]),
            writes=[k["xt"]], dma=True, tag="dxld")
        P.st("vector", lambda e: e.reduce_sum(C["c_sum"][:, :], xt[:, :], AX.X),
             reads=[k["xt"]], writes=[kC["c_sum"]], hazard=True, tag="dsum")
        P.st("scalar", lambda e: e.activation(scr[:, :], xt[:, :], AF.Square,
                                              accum_out=C["c_sq"][:, :]),
             reads=[k["xt"]], writes=[k["scr"], kC["c_sq"]], hazard=True, tag="dsq")
        P.st("vector", lambda e: e.tensor_scalar_mul(C["c_m"][:, :], C["c_sum"][:, :], 1.0 / D),
             reads=[kC["c_sum"]], writes=[kC["c_m"]], hazard=True, tag="dm")
        P.st("vector", lambda e: e.tensor_scalar_mul(C["c_v"][:, :], C["c_sq"][:, :], 1.0 / D),
             reads=[kC["c_sq"]], writes=[kC["c_v"]], hazard=True, tag="dv1")
        P.st("vector", lambda e: e.tensor_mul(C["c_t"][:, :], C["c_m"][:, :], C["c_m"][:, :]),
             reads=[kC["c_m"]], writes=[kC["c_t"]], hazard=True, tag="dm2")
        P.st("vector", lambda e: e.tensor_sub(C["c_v"][:, :], C["c_v"][:, :], C["c_t"][:, :]),
             reads=[kC["c_t"], kC["c_v"]], writes=[kC["c_v"]], hazard=True, tag="dv2")
        P.st("vector", lambda e: e.tensor_scalar_add(C["c_v"][:, :], C["c_v"][:, :], EPS_LN),
             reads=[kC["c_v"]], writes=[kC["c_v"]], hazard=True, tag="dv3")
        P.st("scalar", lambda e: e.activation(C["c_u"][:, :], C["c_v"][:, :], AF.Ln),
             reads=[kC["c_v"]], writes=[kC["c_u"]], hazard=True, tag="dln")
        P.st("scalar", lambda e: e.activation(C["c_rstd"][:, :], C["c_u"][:, :], AF.Exp, scale=-0.5),
             reads=[kC["c_u"]], writes=[kC["c_rstd"]], hazard=True, tag="dexp")
        P.st("vector", lambda e: e.tensor_mul(C["c_nm"][:, :], C["c_m"][:, :], C["c_rstd"][:, :]),
             reads=[kC["c_m"], kC["c_rstd"]], writes=[kC["c_nm"]], hazard=True, tag="dnm")
        P.st("vector", lambda e: e.tensor_scalar_mul(C["c_nm"][:, :], C["c_nm"][:, :], -1.0),
             reads=[kC["c_nm"]], writes=[kC["c_nm"]], hazard=True, tag="dnm2")
        P.st("scalar", lambda e: e.activation(xnb[:, :], xt[:, :], AF.Identity,
                                              bias=C["c_nm"][:, :], scale=C["c_rstd"][:, :]),
             reads=[k["xt"], kC["c_nm"], kC["c_rstd"]], writes=[k["xnb"]], tag="dxn")
        for kd in range(ND):
            P.st("tensor", lambda e, kd2=kd: e.transpose(
                ptb[:, kd2 * 128:(kd2 + 1) * 128], xnb[:, kd2 * 128:(kd2 + 1) * 128],
                eye_b[:, :]),
                reads=[k["xnb"], k["eye_b"]], writes=[k["ptb"]], tag="dtp")
        P.st("vector", lambda e: e.tensor_copy(hx[:, :], ptb[:, :].rearrange("p f -> p f")),
             reads=[k["ptb"]], writes=[k["hx"]], tag="dtc")
        P.st("gpsimd", lambda e, r=rt: e.dma_start(
            out=xgT.rearrange("(kk p) c -> p kk c", p=128)[:, :, r * 128:(r + 1) * 128],
            in_=hx[:, :].rearrange("p (kk c) -> p kk c", kk=ND)),
            reads=[k["hx"]], extra=kxgT.wr(), dma=True, tag="dspill")
        kxgT.did_write(P.steps[-1])

    P.st("vector", lambda e: e.memset(cT[:, :], 0.0), writes=[k["cT"]], tag="c0z")
    for kd in range(ND):
        P.st("vector", lambda e, kd2=kd: e.tensor_scalar_add(
            cT[:, kd2 * RS:(kd2 + 1) * RS], cT[:, kd2 * RS:(kd2 + 1) * RS],
            icc[:, kd2:kd2 + 1]),
            reads=[kicc], writes=[k["cT"]], tag="c0a")
    P.st("vector", lambda e: e.tensor_copy(cTb[:, :], cT[:, :]),
         reads=[k["cT"]], writes=[k["cTb"]], tag="c0b")

    gates = [(wfT, kwfT, ufT, kufT, bfc, kbfc, AF.Sigmoid, fb, k["fb"]),
             (wiT, kwiT, uiT, kuiT, bic, kbic, AF.Sigmoid, ib, k["ib"]),
             (wcT, kwcT, ucT, kucT, bcc, kbcc, AF.Tanh, gb, k["gb"])]
    for t in range(n_steps):
        xs, kxs_ = xstep[t % 2], kxs[t % 2]
        P.st("sync", lambda e, t2=t, x2=xs: e.dma_start(
            out=x2[:, :].rearrange("p (kk c) -> p kk c", kk=ND),
            in_=xgT.rearrange("(kk p) c -> p kk c", p=128)[:, :, t2 * RS:(t2 + 1) * RS]),
            extra=kxgT.rd(), writes=[kxs_], dma=True, tag="dxs")
        kxgT.did_read(P.steps[-1])
        for ct in range(ND):
            for gi, (wT_, kwT_, uT_, kuT_, bc_, kbc_, act, ob, kob) in enumerate(gates):
                pgi, kpgi = pg[gi], k[f"pg{gi}"]
                def gmm(e, c=ct, w2=wT_, u2=uT_, x2=xs, pg2=pgi):
                    for kd in range(ND):
                        e.matmul(pg2[:, 0:RS], w2[:, kd * D + c * 128: kd * D + c * 128 + 128],
                                 x2[:, kd * RS:(kd + 1) * RS],
                                 start=(kd == 0), stop=False)
                    for kd in range(ND):
                        i = e.matmul(pg2[:, 0:RS], u2[:, kd * D + c * 128: kd * D + c * 128 + 128],
                                     cTb[:, kd * RS:(kd + 1) * RS],
                                     start=False, stop=(kd == ND - 1))
                    return i
                P.st("tensor", gmm, reads=[kwT_, kuT_, kxs_, k["cTb"]], writes=[kpgi], tag="gmm")
                P.st("scalar", lambda e, c=ct, b2=bc_, a2=act, o2=ob, pg2=pgi: e.activation(
                    o2[:, c * RS:(c + 1) * RS], pg2[:, 0:RS], a2, bias=b2[:, c:c + 1]),
                    reads=[kpgi, kbc_], writes=[kob], tag="gact")
        for ct in range(ND):
            P.st("vector", lambda e, c=ct: e.tensor_mul(
                tmb[:, :], ib[:, c * RS:(c + 1) * RS], gb[:, c * RS:(c + 1) * RS]),
                reads=[k["ib"], k["gb"]], writes=[k["tmb"]], tag="cupd1")
            P.st("vector", lambda e, c=ct: e.tensor_mul(
                cT[:, c * RS:(c + 1) * RS], fb[:, c * RS:(c + 1) * RS],
                cT[:, c * RS:(c + 1) * RS]),
                reads=[k["fb"]], writes=[k["cT"]], tag="cupd2")
            P.st("vector", lambda e, c=ct: e.tensor_add(
                cT[:, c * RS:(c + 1) * RS], cT[:, c * RS:(c + 1) * RS], tmb[:, :]),
                reads=[k["tmb"]], writes=[k["cT"]], tag="cupd3")
            P.st("vector", lambda e, c=ct: e.tensor_copy(
                cTb[:, c * RS:(c + 1) * RS], cT[:, c * RS:(c + 1) * RS]),
                reads=[k["cT"]], writes=[k["cTb"]], tag="cupd4")

    for rt in range(n_rt):
        b2_, half = rt // 2, rt % 2
        P.st("sync", lambda e, r=rt: e.dma_start(
            out=hx[:, :].rearrange("p (kk c) -> p kk c", kk=ND),
            in_=xgT.rearrange("(kk p) c -> p kk c", p=128)[:, :, r * 128:(r + 1) * 128]),
            extra=kxgT.rd(), writes=[k["hx"]], dma=True, tag="dhx")
        kxgT.did_read(P.steps[-1])
        def homm(e, r=rt, hf=half):
            for ch in range(2):
                for kd in range(ND):
                    e.matmul(ph[:, ch * 512:(ch + 1) * 512],
                             hx[:, kd * 128:(kd + 1) * 128],
                             woT[:, kd * D + ch * 512: kd * D + (ch + 1) * 512],
                             start=(kd == 0), stop=False)
                for kd in range(ND):
                    i = e.matmul(ph[:, ch * 512:(ch + 1) * 512],
                                 cTb[:, kd * RS + hf * 128: kd * RS + hf * 128 + 128],
                                 uoT[:, kd * D + ch * 512: kd * D + (ch + 1) * 512],
                                 start=False, stop=(kd == ND - 1))
            return i
        P.st("tensor", homm, reads=[k["hx"], k["cTb"], kwoT, kuoT], writes=[k["ph"]], tag="homm")
        P.st("vector", lambda e: e.tensor_add(ph[:, :], ph[:, :], boR[:, :]),
             reads=[k["boR"]], writes=[k["ph"]], tag="hbo")
        P.st("scalar", lambda e: e.activation(scr[:, :], ph[:, :], AF.Tanh),
             reads=[k["ph"]], writes=[k["scr"]], tag="htanh")
        if s1_r is None:
            P.st("sync", lambda e, r=rt: e.dma_start(
                out=xt[:, :], in_=attn1_r[r * 128:(r + 1) * 128, :]),
                writes=[k["xt"]], dma=True, tag="datld")
            P.st("vector", lambda e: e.tensor_add(scr[:, :], scr[:, :], xt[:, :]),
                 reads=[k["xt"]], writes=[k["scr"]], tag="hres1")
            P.st("sync", lambda e, r=rt: e.dma_start(
                out=xt[:, :], in_=x1_r[r * 128:(r + 1) * 128, :]),
                writes=[k["xt"]], dma=True, tag="dx1ld")
            P.st("vector", lambda e: e.tensor_add(scr[:, :], scr[:, :], xt[:, :]),
                 reads=[k["xt"]], writes=[k["scr"]], tag="hres2")
        else:
            P.st("sync", lambda e, r=rt: e.dma_start(
                out=xt[:, :], in_=s1_r[r * 128:(r + 1) * 128, :]),
                writes=[k["xt"]], dma=True, tag="ds1ld")
            P.st("vector", lambda e: e.tensor_add(scr[:, :], scr[:, :], xt[:, :]),
                 reads=[k["xt"]], writes=[k["scr"]], tag="hres1")
        P.st("sync", lambda e, r=rt: e.dma_start(
            out=out_r[r * 128:(r + 1) * 128, :], in_=scr[:, :]),
            reads=[k["scr"]], dma=True, tag="dout")


def build_launch_D(n_steps=B, n_rt=NRT):
    cx = Ctx("launchD")
    caps = declare_consts(cx, host_constants())
    x1_r = cx.dram_in("x1_r", [B * RS, D], F32)
    attn1_r = cx.dram_in("attn1_r", [B * RS, D], F32)
    args = {}
    for nm in ["wf", "uf", "wi", "ui", "wc", "uc", "wo", "uo"]:
        args[nm] = cx.dram_in(nm, [D, D], F32)
    for nm in ["bf", "bi", "bc", "bo", "in_c"]:
        args[nm] = cx.dram_in(nm, [1, D], F32)
    out_r = cx.dram_out("out_r", [B * RS, D], F32)
    emit_l1_ffn(cx, caps, x1_r, attn1_r, args["wf"], args["uf"], args["wi"],
                args["ui"], args["wc"], args["uc"], args["wo"], args["uo"],
                args["bf"], args["bi"], args["bc"], args["bo"], args["in_c"],
                out_r, n_steps=n_steps, n_rt=n_rt)
    cx.finish()
    return cx.nc



def build_fused_ABC():
    cx = Ctx("fusedABC")
    caps = declare_consts(cx, host_constants())
    x_sh = cx.dram_in("x_sh", [BL * T, D], F32)
    wq0 = cx.dram_in("wq0", [D, D], F32)
    wk0 = cx.dram_in("wk0", [D, D], F32)
    wv0 = cx.dram_in("wv0", [D, D], F32)
    wo0 = cx.dram_in("wo0", [D, D], F32)
    wqm = cx.dram_in("wqm", [64, 8], F32)
    wkm = cx.dram_in("wkm", [64, 8], F32)
    wmetric = cx.dram_in("wmetric", [8, 64], F32)
    temp = cx.dram_in("temp", [1, 1], F32)
    gamma_x = cx.dram_in("gamma_x", [1, D], F32)
    gamma_slot = cx.dram_in("gamma_slot", [1, D], F32)
    slot_embeds = cx.dram_in("slot_embeds", [NSLOT, D], F32)
    w1 = cx.dram_in("w1", [D, HID], F32)
    w2 = cx.dram_in("w2", [HID, D], F32)
    wq1 = cx.dram_in("wq1", [D, D], F32)
    wk1 = cx.dram_in("wk1", [256, D], F32)
    wv1 = cx.dram_in("wv1", [256, D], F32)
    wo1 = cx.dram_in("wo1", [D, D], F32)
    s1 = cx.dram_out("s1", [BL * T, D], F32)
    xg_sp = cx.dram_out("xg_sp", [D, BL * T], BF16)
    attn0_d = cx.nc.dram_tensor("attn0_d", [BL * T, D], F32).ap()
    x1_d = cx.nc.dram_tensor("x1_d", [BL * T, D], F32).ap()

    cx.begin_phase()
    emit_l0_attn(cx, caps, x_sh, wq0, wk0, wv0, wo0, wqm, wkm, wmetric, temp,
                 attn0_d)
    cx.begin_phase()
    emit_l0_ffn(cx, caps, x_sh, attn0_d, gamma_x, gamma_slot, slot_embeds,
                w1, w2, x1_d)
    cx.begin_phase()
    emit_l1_attn(cx, caps, x1_d, wq1, wk1, wv1, wo1, s1,
                 xg_spill=xg_sp, add_resid=True)
    cx.finish()
    return cx.nc


def build_launch_D2():
    cx = Ctx("launchD2")
    caps = declare_consts(cx, host_constants())
    xg = cx.dram_in("xg", [D, B * RS], BF16)
    s1_r = cx.dram_in("s1_r", [B * RS, D], F32)
    args = {}
    for nm in ["wf", "uf", "wi", "ui", "wc", "uc", "wo", "uo"]:
        args[nm] = cx.dram_in(nm, [D, D], F32)
    for nm in ["bf", "bi", "bc", "bo", "in_c"]:
        args[nm] = cx.dram_in(nm, [1, D], F32)
    out_r = cx.dram_out("out_r", [B * RS, D], F32)
    emit_l1_ffn(cx, caps, None, None, args["wf"], args["uf"], args["wi"],
                args["ui"], args["wc"], args["uc"], args["wo"], args["uo"],
                args["bf"], args["bi"], args["bc"], args["bo"], args["in_c"],
                out_r, xg_in=xg, s1_r=s1_r)
    cx.finish()
    return cx.nc



_BUILT = {}


def _get_built(name, fn):
    if name not in _BUILT:
        _BUILT[name] = fn()
    return _BUILT[name]


def kernel(**inputs):
    consts = host_constants()
    cbase = const_inmap(consts)
    f32 = np.float32

    x = np.asarray(inputs["L0_attn_x"], f32).reshape(B, T, D)

    ncF = _get_built("F", build_fused_ABC)
    baseF = dict(cbase)
    baseF.update({
        "wq0": np.asarray(inputs["L0_attn_Wq"], f32),
        "wk0": np.asarray(inputs["L0_attn_Wk"], f32),
        "wv0": np.asarray(inputs["L0_attn_Wv"], f32),
        "wo0": np.asarray(inputs["L0_attn_Wo"], f32),
        "wqm": np.asarray(inputs["L0_attn_Wqm"], f32),
        "wkm": np.asarray(inputs["L0_attn_Wkm"], f32),
        "wmetric": np.asarray(inputs["L0_attn_W_metric"], f32).reshape(8, 64),
        "temp": np.asarray(inputs["L0_attn_temperature"], f32).reshape(1, 1),
        "gamma_x": np.asarray(inputs["L0_ffn_gamma_x"], f32).reshape(1, D),
        "gamma_slot": np.asarray(inputs["L0_ffn_gamma_slot"], f32).reshape(1, D),
        "slot_embeds": np.asarray(inputs["L0_ffn_slot_embeds"], f32).reshape(NSLOT, D),
        "w1": np.asarray(inputs["L0_ffn_w1"], f32),
        "w2": np.asarray(inputs["L0_ffn_w2"], f32),
        "wq1": np.asarray(inputs["L1_attn_Wq"], f32),
        "wk1": np.asarray(inputs["L1_attn_Wk"], f32),
        "wv1": np.asarray(inputs["L1_attn_Wv"], f32),
        "wo1": np.asarray(inputs["L1_attn_Wo"], f32),
    })
    in_maps = []
    for c in range(NCORES):
        m = dict(baseF)
        m["x_sh"] = np.ascontiguousarray(x[c * BL:(c + 1) * BL].reshape(BL * T, D))
        in_maps.append(m)
    resF = run_bass_kernel_spmd(ncF, in_maps, core_ids=list(range(NCORES)))
    s1 = np.stack([r["s1"].reshape(BL, T, D) for r in resF.results]).reshape(B, T, D)
    xg = np.stack([r["xg_sp"].reshape(D, BL, T) for r in resF.results], 1)
    xg = xg.reshape(D, B, T)

    ncD = _get_built("D2", build_launch_D2)
    baseD = dict(cbase)
    for nm_in, nm in [("L1_ffn_wf", "wf"), ("L1_ffn_uf", "uf"), ("L1_ffn_wi", "wi"),
                      ("L1_ffn_ui", "ui"), ("L1_ffn_wc", "wc"), ("L1_ffn_uc", "uc"),
                      ("L1_ffn_wo", "wo"), ("L1_ffn_uo", "uo")]:
        baseD[nm] = np.asarray(inputs[nm_in], f32)
    for nm_in, nm in [("L1_ffn_bf", "bf"), ("L1_ffn_bi", "bi"), ("L1_ffn_bc", "bc"),
                      ("L1_ffn_bo", "bo"), ("L1_ffn_in_c", "in_c")]:
        baseD[nm] = np.asarray(inputs[nm_in], f32).reshape(1, D)
    in_maps = []
    for c in range(NCORES):
        m = dict(baseD)
        r0 = c * RS
        m["xg"] = np.ascontiguousarray(xg[:, :, r0:r0 + RS].reshape(D, B * RS))
        m["s1_r"] = np.ascontiguousarray(s1[:, r0:r0 + RS, :].reshape(B * RS, D))
        in_maps.append(m)
    resD = run_bass_kernel_spmd(ncD, in_maps, core_ids=list(range(NCORES)))
    out = np.zeros((B, T, D), f32)
    for c in range(NCORES):
        out[:, c * RS:(c + 1) * RS, :] = resD.results[c]["out_r"].reshape(B, RS, D)
    return out
